# version 04
# baseline (speedup 1.0000x reference)
import contextlib
import numpy as np
import ml_dtypes
import concourse.bass as bass
import concourse.mybir as mybir
from concourse.alu_op_type import AluOpType as OP
from concourse.bass_utils import run_bass_kernel_spmd

F32 = mybir.dt.float32
BF16 = mybir.dt.bfloat16
AF = mybir.ActivationFunctionType
AX = mybir.AxisListType

D = 1024
TOK = 2048
NTL = 16
NCH = 8
NES = 64
LN8 = float(np.log(8.0))
LNEPS = 1e-5
FEPS = 1e-6
KB = 1024
BASE = 18432

ts = bass.ts


class Sem:
    __slots__ = ("h", "n")
    def __init__(self, h):
        self.h = h
        self.n = 0
    def inc(self, inst, k=1):
        inst.then_inc(self.h, k)
        self.n += k
        return (self, self.n)
    def tok(self):
        return (self, self.n)


class Bld:
    def __init__(self, nc, blk):
        self.nc = nc
        self.blk = blk
        self.w8 = {}
    def waits(self, ename, eng, toks):
        for t in toks:
            if t is None:
                continue
            s, n = t
            if n <= 0:
                continue
            key = (ename, id(s))
            if self.w8.get(key, -1) >= n:
                continue
            eng.wait_ge(s.h, n)
            self.w8[key] = n
    def on(self, ename, toks, emitfn, inc=None, k=1):
        res = {}
        def _f(eng):
            self.waits(ename, eng, toks)
            last = emitfn(eng)
            if inc is not None:
                res["t"] = inc.inc(last, k)
        getattr(self.blk, ename)(_f)
        return res.get("t")


def build_nc(stage=10, dbg=()):
    nc = bass.Bass("TRN2", target_bir_lowering=False)

    def din(name, shape, dt=F32):
        return nc.dram_tensor(name, list(shape), dt, kind="ExternalInput").ap()

    x_h    = din("x", (TOK, D))
    wqkv_h = din("wqkv", (16, 128, 8, 128))
    wv_h   = din("wv", (128, 8, 8, 128))
    qb_h   = din("qb", (128, 16))
    pw_h   = din("pw", (128, 8, D))
    pbt_h  = din("pbt", (128, D))
    vbt_h  = din("vbt", (128, D))
    pm2_h  = din("pm2", (128, 128))
    c2_h   = din("c2", (128, 8, 128))
    f1_h   = din("f1h", (NES, 128, 512))
    f2_h   = din("f2h", (NES, 64, D))
    fbt_h  = din("fbt", (64, 64))
    id_h   = din("identb", (128, 128))
    mk5_h  = din("mask512", (128, 512))
    mkt_h  = din("maskt", (128, 4, 128))

    out_h = nc.dram_tensor("out", [TOK, D], F32, kind="ExternalOutput").ap()
    dbg_h = {}
    for name, shape in [
        ("d_ln1T", (128, 8, 2048)), ("d_ktqt", (128, 8, 2048)),
        ("d_kp", (128, 2, 1024)), ("d_v", (128, 2, 1024)),
        ("d_kvbd", (128, 8, 132)), ("d_qp", (128, 2, 1024)),
        ("d_attn", (128, 2, 1024)), ("d_ln2", (128, 16, 1024)),
        ("d_pblk", (128, 16, 512)), ("d_pblkt", (128, 16, 4, 128)),
        ("d_selt", (128, 16, 8, 512)), ("d_inner", (64, 64, 128)),
        ("d_inter", (128, 64, 512)),
    ]:
        if name in dbg:
            dbg_h[name] = nc.dram_tensor(name, list(shape), F32, kind="ExternalOutput").ap()

    A = lambda name, shape, dt, off: nc.alloc_sbuf_tensor_at(name, list(shape), dt, offset=BASE + off).ap()
    x_sb     = A("x_sb",     [128, NTL, D], F32, 0)
    selt_sb  = A("selt_sb",  [128, NTL, NCH, 512], BF16, 0)
    inter_sb = A("inter_sb", [128, NES, 512], BF16, 0)
    ln1T_sb  = A("ln1T_sb",  [128, NCH, TOK], BF16, 64 * KB)
    pwT_sb   = A("pwT_sb",   [128, NCH, D], BF16, 64 * KB)
    ktqt_sb  = A("ktqt_sb",  [128, NCH, TOK], BF16, 96 * KB)
    sq2_sb   = A("sq2_sb",   [128, D], F32, 96 * KB)
    ln2_sb   = A("ln2_sb",   [128, NTL, D], BF16, 128 * KB)
    inner_sb = A("inner_sb", [64, NES, 128], BF16, 128 * KB)
    pblk_sb  = A("pblk_sb",  [128, NTL, 512], BF16, 160 * KB)
    pblkT_sb = A("pblkT_sb", [128, NTL, 4, 128], BF16, 176 * KB)

    o = 80 * KB
    ekq_sb   = A("ekq_sb",   [128, D], BF16, o); o += 2 * KB
    qp_sb    = A("qp_sb",    [128, D], BF16, o); o += 2 * KB
    qpT_sb   = A("qpT_sb",   [128, D], BF16, o); o += 2 * KB
    attn_sb  = A("attn_sb",  [128, D], BF16, o); o += 2 * KB
    attnT_sb = A("attnT_sb", [128, D], BF16, o); o += 2 * KB
    ln2T_sb  = A("ln2T_sb",  [128, D], BF16, o); o += 2 * KB
    sq7_sb   = A("sq7_sb",   [128, D], BF16, o); o += 2 * KB
    mk5_sb   = A("mk5_sb",   [128, 512], BF16, o); o += 1 * KB
    mkt_sb   = A("mkt_sb",   [128, 4, 128], BF16, o); o += 1 * KB
    assert o <= 96 * KB

    f1big_sb = A("f1big_sb", [128, 2, 8, 512], BF16, 160 * KB)
    f2big_sb = A("f2big_sb", [64, 2, 8, 512], BF16, 144 * KB)
    stack_sb = A("stack_sb", [128, 8, 512], BF16, 96 * KB)
    x2re_sb  = A("x2re_sb",  [128, 4, 512], F32, 104 * KB)
    fin_sb   = A("fin_sb",   [128, 4, 512], F32, 112 * KB)

    o = 160 * KB
    wslab_sb = A("wslab_sb", [128, 2, NCH, 128], BF16, o); o += 4 * KB
    wv_sb    = A("wv_sb",    [128, NCH, 8, 128], BF16, o); o += 16 * KB
    vtile_sb = A("vtile_sb", [128, 2, D], BF16, o); o += 4 * KB
    kp_sb    = A("kp_sb",    [128, 2, D], BF16, o); o += 4 * KB
    ekk_sb   = A("ekk_sb",   [128, D], BF16, o); o += 2 * KB
    zro_sb   = A("zro_sb",   [128, 128], BF16, o); o += 256
    assert o <= 192 * KB

    o = 192 * KB
    def DA(name, shape, dt):
        nonlocal o
        t = A(name, shape, dt, o)
        per = int(np.prod(shape[1:])) * mybir.dt.size(dt)
        o += (per + 31) // 32 * 32
        return t
    id_sb   = DA("id_sb",   [128, 128], BF16)
    pm2_sb  = DA("pm2_sb",  [128, 128], BF16)
    c2_sb   = DA("c2_sb",   [128, NCH, 128], BF16)
    qb_sb   = DA("qb_sb",   [128, 16], F32)
    fbt_sb  = DA("fbt_sb",  [64, 64], F32)
    kvbd_sb = DA("kvbd_sb", [128, 8, 132], BF16)
    ones_sb = DA("ones_sb", [128, 2], BF16)
    st_sb   = DA("st_sb",   [128, 16], F32)
    mx_sb   = DA("mx_sb",   [128, 16], F32)
    nrm_sb  = DA("nrm_sb",  [128, 32], F32)
    gm_sb   = DA("gm_sb",   [128, 8], F32)
    pt_sb   = DA("pt_sb",   [128, 128], BF16)
    ptok_sb = DA("ptok_sb", [128, 128], BF16)
    pbt_sb  = DA("pbt_sb",  [128, D], F32)
    vbt_sb  = DA("vbt_sb",  [128, D], F32)
    eps_sb  = DA("eps_sb",  [128, 1], F32)
    ln8_sb  = DA("ln8_sb",  [128, 1], F32)
    feps_sb = DA("feps_sb", [128, 1], F32)
    mx2_sb  = DA("mx2_sb",  [128, 16], F32)
    assert o <= 206 * KB, o

    PA = nc.alloc_psum_tensor("PA", [128, 1024], F32).ap()
    PB = nc.alloc_psum_tensor("PB", [128, 1024], F32).ap()
    PC = nc.alloc_psum_tensor("PC", [128, 1024], F32).ap()
    PT1 = nc.alloc_psum_tensor("PT1", [128, 1024], BF16).ap()
    PS = nc.alloc_psum_tensor("PS", [128, 512], F32).ap()
    PQ = [PA[:, 0:512], PA[:, 512:1024], PB[:, 0:512], PB[:, 512:1024]]

    stack_cm = contextlib.ExitStack()
    sems = {}
    for nm in ["sT", "sV", "sS", "sG", "dC", "dCG", "dX", "dWV", "dPW", "dSP", "dDB",
               "dW0", "dW1", "dF0", "dF1", "dG0", "dG1",
               "dRL0", "dRL1", "dRL2", "dRL3", "dRL4", "dRL5", "dRL6", "dRL7",
               "dXR0", "dXR1", "dXR2", "dXR3", "dOU0", "dOU1", "dOU2", "dOU3"]:
        sems[nm] = Sem(stack_cm.enter_context(nc.semaphore(nm)))
    sT, sV, sS, sG = sems["sT"], sems["sV"], sems["sS"], sems["sG"]
    dC, dCG, dX, dWV, dPW, dSP, dDB = (sems[n] for n in ["dC", "dCG", "dX", "dWV", "dPW", "dSP", "dDB"])
    dW = [sems["dW0"], sems["dW1"]]
    dF = [sems["dF0"], sems["dF1"]]
    dG2 = [sems["dG0"], sems["dG1"]]
    dRL = [sems["dRL%d" % i] for i in range(8)]
    dXR = [sems["dXR%d" % i] for i in range(4)]
    dOU = [sems["dOU%d" % i] for i in range(4)]

    with nc.Block() as blk:
        b = Bld(nc, blk)

        def fin():
            toks = [s.tok() for s in sems.values()]
            b.on("sync", toks, lambda e: e.nop())

        def dbg_dump(name, src_ap, toks):
            if name in dbg_h:
                b.on("gpsimd", toks, lambda e: e.dma_start(out=dbg_h[name], in_=src_ap), dDB, 16)

        def _ld_sync(e):
            for t in range(NTL):
                e.dma_start(out=x_sb[:, t, :], in_=x_h[ts(t, 128), :]).then_inc(dX.h, 16)
            for dst, src in [(qb_sb, qb_h), (fbt_sb, fbt_h), (vbt_sb, vbt_h)]:
                e.dma_start(out=dst[:], in_=src[:]).then_inc(dC.h, 16)
            return e.dma_start(out=pbt_sb[:], in_=pbt_h[:])
        dX.n += 16 * 16
        dC.n += 3 * 16
        tok_ld = b.on("sync", [], _ld_sync, dC, 16)
        tok_x = (dX, dX.n)

        def _ld_g(e):
            e.dma_start(out=pm2_sb[:], in_=pm2_h[:]).then_inc(dCG.h, 16)
            e.dma_start(out=id_sb[:], in_=id_h[:]).then_inc(dCG.h, 16)
            return e.dma_start(out=c2_sb[:], in_=c2_h[:])
        dCG.n += 32
        tok_ldg = b.on("gpsimd", [], _ld_g, dCG, 16)
        tok_pm2 = b.on("scalar", [tok_ldg], lambda e: e.mul(pm2_sb[:], pm2_sb[:], 0.125), sS)
        def _ones(e):
            e.memset(ones_sb[:], 1.0)
            e.memset(eps_sb[:], LNEPS)
            e.memset(ln8_sb[:], LN8)
            e.memset(zro_sb[:], 0.0)
            return e.memset(feps_sb[:], FEPS)
        tok_ones = b.on("vector", [], _ones, sV)

        ln_hist = {"mu": [None, None], "var": [None, None], "ln": [None, None],
                   "n": [None, None]}
        def ln_tile(xt, out_ap, t, sq_scratch, extra=()):
            p = t % 2
            tok_sq = b.on("scalar", [ln_hist["var"][p]] + list(extra),
                lambda e: e.activation(sq_scratch[:], xt, AF.Square,
                                       accum_out=st_sb[:, p+2:p+3]), sS)
            tok_r = b.on("vector", [ln_hist["mu"][p]] + list(extra),
                lambda e: e.tensor_reduce(st_sb[:, p:p+1], xt, axis=AX.X, op=OP.add), sV)
            tok_mu = b.on("scalar", [tok_r, ln_hist["n"][p]],
                lambda e: e.mul(st_sb[:, p+4:p+5], st_sb[:, p:p+1], 1.0 / D), sS)
            ln_hist["mu"][p] = tok_mu
            tok_mus = b.on("scalar", [tok_mu],
                lambda e: e.mul(st_sb[:, p+6:p+7], st_sb[:, p+4:p+5], st_sb[:, p+4:p+5]), sS)
            tok_var = b.on("vector", [tok_mus, tok_sq, ln_hist["ln"][p]],
                lambda e: e.tensor_scalar(st_sb[:, p+8:p+9], st_sb[:, p+2:p+3], 1.0 / D,
                                          st_sb[:, p+6:p+7], OP.mult, OP.subtract), sV)
            ln_hist["var"][p] = tok_var
            tok_ln = b.on("scalar", [tok_var, ln_hist["n"][p], tok_ones],
                lambda e: e.activation(st_sb[:, p+10:p+11], st_sb[:, p+8:p+9], AF.Ln,
                                       bias=eps_sb[:, 0:1]), sS)
            ln_hist["ln"][p] = tok_ln
            tok_rs = b.on("scalar", [tok_ln],
                lambda e: e.activation(st_sb[:, p+10:p+11], st_sb[:, p+10:p+11], AF.Exp,
                                       scale=-0.5), sS)
            tok_n = b.on("vector", [tok_rs],
                lambda e: e.tensor_scalar(out_ap, xt, st_sb[:, p+4:p+5], st_sb[:, p+10:p+11],
                                          OP.subtract, OP.mult), sV)
            ln_hist["n"][p] = tok_n
            return tok_n

        tok_tp_prev = None
        lnt_last = None
        for t in range(NTL):
            xt = x_sb[:, t, :]
            tok_n = ln_tile(xt, ekk_sb[:], t, sq2_sb, extra=[tok_x, tok_tp_prev])
            def _tt(e):
                last = None
                for c in range(NCH):
                    last = e.transpose(PT1[:, ts(c, 128)], ekk_sb[:, ts(c, 128)], id_sb[:])
                return last
            tok_tp = b.on("tensor", [tok_n, tok_ld, lnt_last], _tt, sT)
            tok_tp_prev = tok_tp
            lnt_last = b.on("vector", [tok_tp],
                lambda e, t=t: e.tensor_copy(
                    ln1T_sb[:, :, ts(t, 128)],
                    PT1[:].rearrange("p (c q) -> p c q", c=NCH)), sV)
        tok_ln1T = lnt_last
        dbg_dump("d_ln1T", ln1T_sb[:], [tok_ln1T])
        if stage < 3:
            fin(); stack_cm.close(); return nc

        def qkv_phase(jlist, barrier, seed_mm=()):
            evac_toks, mm_toks = [], list(seed_mm)
            nseed = len(mm_toks)
            for jj, j in enumerate(jlist):
                buf = jj % 2
                free_tok = mm_toks[nseed + jj - 2] if nseed + jj >= 2 else None
                tok_w = b.on("gpsimd", [free_tok, tok_ln1T],
                    lambda e, j=j, buf=buf: e.dma_start(out=wslab_sb[:, buf, :, :],
                                                        in_=wqkv_h[j, :, :, :]), dW[buf], 16)
                def _mm(e, buf=buf):
                    last = None
                    for q in range(4):
                        for c in range(NCH):
                            last = e.matmul(PQ[q], wslab_sb[:, buf, c, :],
                                            ln1T_sb[:, c, ts(q, 512)],
                                            start=(c == 0), stop=(c == NCH - 1),
                                            skip_group_check=True)
                    return last
                prev_evac = evac_toks[jj - 1] if jj >= 1 else None
                tok_mm = b.on("tensor", [tok_w, tok_ln1T, prev_evac] + barrier, _mm, sT)
                mm_toks.append(tok_mm)
                def _ev(e, jj=jj, j=j):
                    e.activation(ktqt_sb[:, jj, 0:1024], PA[:], AF.Identity,
                                 bias=qb_sb[:, j:j+1], scale=1.0)
                    return e.activation(ktqt_sb[:, jj, 1024:2048], PB[:], AF.Identity,
                                        bias=qb_sb[:, j:j+1], scale=1.0)
                evac_toks.append(b.on("scalar", [tok_mm], _ev, sS))
            return evac_toks[-1], mm_toks

        tok_kt, k_mm_toks = qkv_phase(range(8, 16), [])
        if stage == 3:
            dbg_dump("d_ktqt", ktqt_sb[:], [tok_kt])
            fin(); stack_cm.close(); return nc

        tok_wv = b.on("gpsimd", [], lambda e: e.dma_start(out=wv_sb[:], in_=wv_h[:]), dWV, 16)
        def _open(e):
            e.matmul(PC[:, 0:512], zro_sb[:], ktqt_sb[:, 0, 0:512], start=True, stop=True,
                     skip_group_check=True)
            e.matmul(PC[:, 512:1024], zro_sb[:], ktqt_sb[:, 0, 0:512], start=True, stop=True,
                     skip_group_check=True)
            return e.matmul(PS[:, 0:8], zro_sb[:], ktqt_sb[:, 0, 0:8], start=True, stop=True,
                            skip_group_check=True)
        tok_open = b.on("tensor", [tok_kt, tok_ones], _open, sT)
        tok_sub_prev = None
        tok_vev_prev = None
        toks_mkv = [None] * NTL
        tok_kp_last = None
        tok_vt_last = None
        for t in range(NTL):
            pg = t % 2
            def _mmc(e, t=t):
                last = None
                for c in range(NCH):
                    last = e.matmul(PA[:, ts(c, 128)], ktqt_sb[:, c, ts(t, 128)], pm2_sb[:],
                                    start=True, stop=True, skip_group_check=True)
                return last
            tok_mmc = b.on("tensor", [tok_kt, tok_pm2, tok_sub_prev], _mmc, sT)
            tok_red = b.on("vector", [tok_mmc],
                lambda e: e.reduce_max(mx_sb[:], PA[:].rearrange("p (h m) -> p h m", h=16),
                                       axis=AX.X), sV)
            tok_mx2 = b.on("scalar", [tok_red, tok_ones],
                lambda e: e.add(mx2_sb[:], mx_sb[:], ln8_sb[:, 0:1]), sS)
            tok_sub = b.on("vector", [tok_mx2],
                lambda e: e.scalar_tensor_tensor(
                    ekk_sb[:].rearrange("p (h m) -> p h m", h=16),
                    PA[:].rearrange("p (h m) -> p h m", h=16), 1.0,
                    mx2_sb[:].unsqueeze(2).broadcast_to([128, 16, 64]),
                    OP.mult, OP.subtract), sV)
            tok_sub_prev = tok_sub
            tok_exp = b.on("scalar", [tok_sub], lambda e: e.activation(ekk_sb[:], ekk_sb[:], AF.Exp), sS)
            tok_kp = b.on("vector", [tok_exp, toks_mkv[t - 2] if t >= 2 else None],
                lambda e, pg=pg: e.tensor_scalar(kp_sb[:, pg, :], ekk_sb[:], FEPS, None, OP.add), sV)
            tok_kp_last = tok_kp
            def _mmv(e, t=t):
                last = None
                for q in range(2):
                    for c in range(NCH):
                        last = e.matmul(PB[:, ts(q, 512)], ln1T_sb[:, c, ts(t, 128)],
                                        wv_sb[:, c, ts(q * 4, 4) if False else slice(4*q, 4*q+4), :],
                                        start=(c == 0), stop=(c == NCH - 1), skip_group_check=True)
                return last
            tok_mmv = b.on("tensor", [tok_wv, tok_vev_prev], _mmv, sT)
            tok_vt = b.on("vector", [tok_mmv, toks_mkv[t - 2] if t >= 2 else None],
                lambda e, pg=pg: e.scalar_tensor_tensor(vtile_sb[:, pg, :], PB[:], 1.0,
                                                        vbt_sb[:], OP.mult, OP.add), sV)
            tok_vev_prev = tok_vt
            tok_vt_last = tok_vt
            def _mkv(e, t=t, pg=pg):
                last = None
                for p in range(8):
                    e.matmul(PC[:, ts(p, 128)], kp_sb[:, pg, ts(p, 128)],
                             vtile_sb[:, pg, ts(p, 128)],
                             start=False, stop=(t == NTL - 1), skip_group_check=True)
                    last = e.matmul(PS[:, p:p+1], kp_sb[:, pg, ts(p, 128)], ones_sb[:, 0:1],
                                    start=False, stop=(t == NTL - 1), skip_group_check=True)
                return last
            toks_mkv[t] = b.on("tensor", [tok_kp, tok_vt, tok_open], _mkv, sT)
        tok_kv = toks_mkv[-1]
        dbg_dump("d_kp", kp_sb[:], [tok_kp_last])
        dbg_dump("d_v", vtile_sb[:], [tok_vt_last])

        def _kvbd(e):
            e.memset(kvbd_sb[:], 0.0)
            for p in range(8):
                e.tensor_copy(kvbd_sb[0:64, p, 0:64], PC[0:64, 128*p:128*p+64])
                e.tensor_copy(kvbd_sb[64:128, p, 64:128], PC[64:128, 128*p+64:128*p+128])
                e.tensor_copy(kvbd_sb[0:64, p, 128:129], PS[0:64, p:p+1])
                e.tensor_copy(kvbd_sb[64:128, p, 129:130], PS[64:128, p:p+1])
            return e.nop()
        tok_kvbd = b.on("vector", [tok_kv], _kvbd, sV)
        dbg_dump("d_kvbd", kvbd_sb[:], [tok_kvbd])
        if stage < 6:
            fin(); stack_cm.close(); return nc

        tok_qt, q_mm_toks = qkv_phase(range(0, 8), [tok_sub_prev, tok_vev_prev, tok_kvbd],
                                      seed_mm=k_mm_toks[-2:])
        def _pwmk(e):
            e.dma_start(out=mk5_sb[:], in_=mk5_h[:]).then_inc(dPW.h, 16)
            e.dma_start(out=mkt_sb[:], in_=mkt_h[:]).then_inc(dPW.h, 16)
            return e.dma_start(out=pwT_sb[:], in_=pw_h[:])
        dPW.n += 32
        tok_pw = b.on("gpsimd", [q_mm_toks[-1]], _pwmk, dPW, 16)
        if stage >= 6:
            dbg_dump("d_ktqt", ktqt_sb[:], [tok_qt])
        if stage < 7:
            fin(); stack_cm.close(); return nc

        tok_subq_prev = None
        spill_toks = [None] * NTL
        ln2_toks = [None] * NTL
        pblk_toks = [None] * NTL
        tok_mo_prev = None
        tok_attn_prev = None
        tok_mp_prev = None
        tok_x2_prev = None
        tok_mlg_prev = None
        tok_tq_prev = None
        tok_ta_prev = None
        for t in range(NTL):
            def _mmc(e, t=t):
                last = None
                for c in range(NCH):
                    last = e.matmul(PA[:, ts(c, 128)], ktqt_sb[:, c, ts(t, 128)], pm2_sb[:],
                                    start=True, stop=True, skip_group_check=True)
                return last
            tok_mmc = b.on("tensor", [tok_qt, tok_subq_prev], _mmc, sT)
            tok_redq = b.on("vector", [tok_mmc],
                lambda e: e.reduce_max(mx_sb[:], PA[:].rearrange("p (h m) -> p h m", h=16),
                                       axis=AX.X), sV)
            tok_mx2q = b.on("scalar", [tok_redq, tok_ones],
                lambda e: e.add(mx2_sb[:], mx_sb[:], ln8_sb[:, 0:1]), sS)
            tok_subq = b.on("vector", [tok_mx2q],
                lambda e: e.scalar_tensor_tensor(
                    ekq_sb[:].rearrange("p (h m) -> p h m", h=16),
                    PA[:].rearrange("p (h m) -> p h m", h=16), 1.0,
                    mx2_sb[:].unsqueeze(2).broadcast_to([128, 16, 64]),
                    OP.mult, OP.subtract), sV)
            tok_subq_prev = tok_subq
            tok_expq = b.on("scalar", [tok_subq], lambda e: e.activation(ekq_sb[:], ekq_sb[:], AF.Exp), sS)
            tok_qp = b.on("vector", [tok_expq, tok_tq_prev],
                lambda e: e.tensor_scalar(qp_sb[:], ekq_sb[:], FEPS, None, OP.add), sV)
            def _tq(e):
                last = None
                for c in range(NCH):
                    last = e.transpose(PT1[:, ts(c, 128)], qp_sb[:, ts(c, 128)], id_sb[:])
                return last
            tok_tq = b.on("tensor", [tok_qp], _tq, sT)
            tok_tq_prev = tok_tq
            tok_qpT = b.on("vector", [tok_tq, tok_mo_prev],
                lambda e: e.tensor_copy(qpT_sb[:], PT1[:]), sV)
            def _mo(e):
                last = None
                for p in range(8):
                    e.matmul(PB[:, ts(p, 128)], qpT_sb[:, ts(p, 128)], kvbd_sb[:, p, 0:128],
                             start=True, stop=True, skip_group_check=True)
                    last = e.matmul(PS[:, 32+2*p:32+2*p+2], qpT_sb[:, ts(p, 128)],
                                    kvbd_sb[:, p, 128:130],
                                    start=True, stop=True, skip_group_check=True)
                return last
            tok_mo = b.on("tensor", [tok_qpT, tok_kvbd, tok_attn_prev], _mo, sT)
            tok_mo_prev = tok_mo
            tok_ne = b.on("scalar", [tok_mo, tok_ones],
                lambda e: e.add(nrm_sb[:, 0:16], PS[:, 32:48], feps_sb[:, 0:1]), sS)
            tok_rcn = b.on("vector", [tok_ne],
                lambda e: e.reciprocal(nrm_sb[:, 16:32], nrm_sb[:, 0:16]), sV)
            tok_attn = b.on("vector", [tok_rcn, tok_ta_prev],
                lambda e: e.scalar_tensor_tensor(
                    attn_sb[:].rearrange("p (h m) -> p h m", h=16),
                    PB[:].rearrange("p (h m) -> p h m", h=16), 1.0,
                    nrm_sb[:, 16:32].unsqueeze(2).broadcast_to([128, 16, 64]),
                    OP.mult, OP.mult), sV)
            tok_attn_prev = tok_attn
            if t in (0, 1):
                if "d_qp" in dbg_h:
                    b.on("gpsimd", [tok_qp], lambda e, t=t: e.dma_start(out=dbg_h["d_qp"][:, t, :], in_=qp_sb[:]), dDB, 16)
                if "d_attn" in dbg_h:
                    b.on("gpsimd", [tok_attn], lambda e, t=t: e.dma_start(out=dbg_h["d_attn"][:, t, :], in_=attn_sb[:]), dDB, 16)
            def _ta(e):
                last = None
                for c in range(NCH):
                    last = e.transpose(PT1[:, ts(c, 128)], attn_sb[:, ts(c, 128)], id_sb[:])
                return last
            tok_ta = b.on("tensor", [tok_attn], _ta, sT)
            tok_ta_prev = tok_ta
            tok_aT = b.on("vector", [tok_ta, tok_mp_prev],
                lambda e: e.tensor_copy(attnT_sb[:], PT1[:]), sV)
            def _mp(e):
                last = None
                for q in range(2):
                    for c in range(NCH):
                        last = e.matmul(PC[:, ts(q, 512)], attnT_sb[:, ts(c, 128)],
                                        pwT_sb[:, c, ts(q, 512)],
                                        start=(c == 0), stop=(c == NCH - 1), skip_group_check=True)
                return last
            w_mp = [tok_aT, tok_pw, tok_x2_prev] + ([tok_kvbd] if t == 0 else [])
            tok_mp = b.on("tensor", w_mp, _mp, sT)
            tok_mp_prev = tok_mp
            xt = x_sb[:, t, :]
            tok_xpb = b.on("gpsimd", [tok_ld],
                lambda e, xt=xt: e.tensor_tensor(xt, xt, pbt_sb[:], OP.add), sG)
            tok_x2 = b.on("vector", [tok_mp, tok_xpb],
                lambda e, xt=xt: e.tensor_tensor(xt, PC[:], xt, OP.add), sV)
            tok_x2_prev = tok_x2
            spill_toks[t] = b.on("sync", [tok_x2],
                lambda e, xt=xt, t=t: e.dma_start(out=out_h[ts(t, 128), :], in_=xt), dSP, 16)
            ln2_toks[t] = ln_tile(xt, ln2_sb[:, t, :], t, sq7_sb, extra=[tok_x2])
            def _tl(e, t=t):
                last = None
                for c in range(NCH):
                    last = e.transpose(PT1[:, ts(c, 128)], ln2_sb[:, t, ts(c, 128)], id_sb[:])
                return last
            tok_tl = b.on("tensor", [ln2_toks[t]], _tl, sT)
            tok_l2T = b.on("vector", [tok_tl, tok_mlg_prev],
                lambda e: e.tensor_copy(ln2T_sb[:], PT1[:]), sV)
            def _ml(e):
                last = None
                for c in range(NCH):
                    last = e.matmul(PS[:, 128:256], c2_sb[:, c, :], ln2T_sb[:, ts(c, 128)],
                                    start=(c == 0), stop=(c == NCH - 1), skip_group_check=True)
                return last
            tok_mlg = b.on("tensor", [tok_l2T, tok_ldg, tok_mlg_prev], _ml, sT)
            tok_mlg_prev = tok_mlg
            tok_gm = b.on("vector", [tok_mlg],
                lambda e: e.reduce_max(gm_sb[:], PS[:, 128:256].rearrange("p (g q) -> p g q", g=8),
                                       axis=AX.X), sV)
            tok_pt = b.on("vector", [tok_gm],
                lambda e: e.tensor_tensor(pt_sb[:].rearrange("p (g q) -> p g q", g=8),
                                          PS[:, 128:256].rearrange("p (g q) -> p g q", g=8),
                                          gm_sb[:].unsqueeze(2).broadcast_to([128, 8, 16]),
                                          OP.is_equal), sV)
            tok_tpt = b.on("tensor", [tok_pt],
                lambda e: e.transpose(PT1[:, 0:128], pt_sb[:], id_sb[:]), sT)
            tok_ptok = b.on("vector", [tok_tpt],
                lambda e: e.tensor_copy(ptok_sb[:], PT1[:, 0:128]), sV)
            def _vb(e, t=t):
                e.tensor_tensor(pblk_sb[:, t, :].rearrange("p (g q) -> p g q", g=8),
                                ptok_sb[:, 0:64].unsqueeze(1).broadcast_to([128, 8, 64]),
                                mk5_sb[:].rearrange("p (g q) -> p g q", g=8), OP.mult)
                last = None
                for c in range(4):
                    last = e.tensor_tensor(pblkT_sb[:, t, c, :], pt_sb[:], mkt_sb[:, c, :],
                                           OP.mult)
                return last
            pblk_toks[t] = b.on("vector", [tok_ptok, tok_pw], _vb, sV)

        dbg_dump("d_ln2", ln2_sb[:], [ln2_toks[-1]])
        dbg_dump("d_pblk", pblk_sb[:], [pblk_toks[-1]])
        dbg_dump("d_pblkt", pblkT_sb[:], [pblk_toks[-1]])
        if stage < 8:
            fin(); stack_cm.close(); return nc

        barrier8 = [spill_toks[-1], pblk_toks[-1], sT.tok(), sV.tok(), sS.tok(), sG.tok()]
        ev_toks = []
        for t in range(NTL):
            for c in range(NCH):
                i = t * NCH + c
                q = i % 4
                free_tok = ev_toks[-4] if len(ev_toks) >= 4 else None
                tok_g = b.on("tensor", barrier8 + [pblk_toks[t], ln2_toks[t], free_tok],
                    lambda e, t=t, c=c, q=q: e.matmul(PQ[q], ln2_sb[:, t, ts(c, 128)],
                                                      pblk_sb[:, t, :], start=True, stop=True,
                                                      skip_group_check=True), sT)
                if i % 2 == 0:
                    ev_toks.append(b.on("vector", [tok_g],
                        lambda e, t=t, c=c, q=q: e.tensor_copy(selt_sb[:, t, c, :], PQ[q]), sV))
                else:
                    ev_toks.append(b.on("scalar", [tok_g],
                        lambda e, t=t, c=c, q=q: e.activation(selt_sb[:, t, c, :], PQ[q], AF.Copy), sS))
        tok_selt = [sV.tok(), sS.tok()]
        dbg_dump("d_selt", selt_sb[:], tok_selt)
        if stage < 9:
            fin(); stack_cm.close(); return nc

        f1_mm_toks, relu_toks = [], []
        blk_last_mm = []
        for blkI in range(8):
            buf = blkI % 2
            free_tok = blk_last_mm[blkI - 2] if blkI >= 2 else None
            tok_w = b.on("gpsimd", [free_tok] + tok_selt,
                lambda e, blkI=blkI, buf=buf: e.dma_start(
                    out=f1big_sb[:, buf, :, :],
                    in_=f1_h[8*blkI:8*blkI+8, :, :].transpose([1, 0, 2])), dF[buf], 16)
            for e8 in range(8):
                es = 8 * blkI + e8
                psl = PA[0:64, 512 * (es % 2):512 * (es % 2) + 128]
                def _mf(e, es=es, buf=buf, e8=e8, psl=psl):
                    last = None
                    for c in range(NCH):
                        last = e.matmul(psl, f1big_sb[:, buf, e8, ts(c, 64)],
                                        selt_sb[:, :, c, es:es + 64 * 7 + 1:64],
                                        start=(c == 0), stop=(c == NCH - 1),
                                        skip_group_check=True)
                    return last
                prev_relu = relu_toks[es - 2] if es >= 2 else None
                tok_mf = b.on("tensor", [tok_w, prev_relu] + tok_selt, _mf, sT)
                f1_mm_toks.append(tok_mf)
                relu_toks.append(b.on("scalar", [tok_mf],
                    lambda e, es=es, psl=psl: e.activation(inner_sb[:, es, :], psl, AF.Relu,
                                                           bias=fbt_sb[:, es:es+1], scale=1.0), sS))
            blk_last_mm.append(f1_mm_toks[-1])
        tok_inner = relu_toks[-1]
        dbg_dump("d_inner", inner_sb[:], [tok_inner])
        if stage < 10:
            fin(); stack_cm.close(); return nc

        barrier10 = [tok_inner, f1_mm_toks[-1], sT.tok()]
        fin_toks = [None] * NTL
        outdma_toks = [None] * NTL
        for h in range(2):
            f2ev_toks, f2mm_toks = [], []
            blk2_last = []
            for blkI in range(8):
                buf = blkI % 2
                free_tok = blk2_last[blkI - 2] if blkI >= 2 else None
                tok_w = b.on("gpsimd", [free_tok] + barrier10,
                    lambda e, blkI=blkI, buf=buf, h=h: e.dma_start(
                        out=f2big_sb[:, buf, :, :],
                        in_=f2_h[8*blkI:8*blkI+8, :, ts(h, 512)].transpose([1, 0, 2])), dG2[buf], 16)
                for e8 in range(8):
                    es = 8 * blkI + e8
                    q = es % 4
                    prev_ev = f2ev_toks[es - 4] if es >= 4 else None
                    tok_m2 = b.on("tensor", [tok_w, prev_ev] + barrier10,
                        lambda e, es=es, buf=buf, e8=e8, q=q: e.matmul(
                            PQ[q], inner_sb[:, es, :], f2big_sb[:, buf, e8, :], start=True,
                            stop=True, skip_group_check=True), sT)
                    f2mm_toks.append(tok_m2)
                    if es % 2 == 0:
                        f2ev_toks.append(b.on("vector", [tok_m2],
                            lambda e, es=es, q=q: e.tensor_copy(inter_sb[:, es, :], PQ[q]), sV))
                    else:
                        f2ev_toks.append(b.on("scalar", [tok_m2],
                            lambda e, es=es, q=q: e.activation(inter_sb[:, es, :], PQ[q], AF.Copy), sS))
                blk2_last.append(f2mm_toks[-1])
            tok_inter = [sV.tok(), sS.tok()]
            if h == 0:
                dbg_dump("d_inter", inter_sb[:], tok_inter)
            stk_mm_all = []
            for t in range(NTL):
                tok_x2re = b.on("scalar", [dSP.tok(), fin_toks[t - 4] if t >= 4 else None],
                    lambda e, t=t, h=h: e.dma_start(out=x2re_sb[:, t % 4, :],
                                                    in_=out_h[ts(t, 128), ts(h, 512)]), dXR[t % 4], 16)
                for c in range(4):
                    g0 = 8 * t + 2 * c
                    i = 4 * t + c
                    sbuf_i = i % 8
                    prev_use = stk_mm_all[i - 8] if i >= 8 else None
                    qeng = "sync" if c % 2 == 0 else "scalar"
                    tok_rl = b.on(qeng, tok_inter + [prev_use],
                        lambda e, g0=g0, sbuf_i=sbuf_i: e.dma_start(out=stack_sb[:, sbuf_i, :],
                                                                    in_=inter_sb[g0:g0+2, :, :]),
                        dRL[sbuf_i], 16)
                    tok_ms = b.on("tensor", [tok_rl, pblk_toks[t],
                                             fin_toks[t - 2] if t >= 2 else None],
                        lambda e, t=t, c=c, sbuf_i=sbuf_i: e.matmul(
                            PC[:, ts(t % 2, 512)], pblkT_sb[:, t, c, :],
                            stack_sb[:, sbuf_i, :], start=(c == 0),
                            stop=(c == 3), skip_group_check=True), sT)
                    stk_mm_all.append(tok_ms)
                tok_scat = stk_mm_all[-1]
                tok_fin = b.on("vector", [tok_scat, tok_x2re, outdma_toks[t - 4] if t >= 4 else None],
                    lambda e, t=t: e.tensor_tensor(fin_sb[:, t % 4, :], PC[:, ts(t % 2, 512)],
                                                   x2re_sb[:, t % 4, :], OP.add), sV)
                fin_toks[t] = tok_fin
                outdma_toks[t] = b.on("sync", [tok_fin],
                    lambda e, t=t, h=h: e.dma_start(out=out_h[ts(t, 128), ts(h, 512)],
                                                    in_=fin_sb[:, t % 4, :]), dOU[t % 4], 16)
            barrier10 = [tok_inner, sT.tok(), dRL[0].tok(), dRL[1].tok()]

        fin()
        stack_cm.close()
    return nc


def _prep(inputs):
    x = np.asarray(inputs["L0_attn_x"], dtype=np.float32)
    qkv_w = np.ascontiguousarray(np.asarray(inputs["L0_attn_qkv_w"], dtype=np.float32))
    qkv_b = np.asarray(inputs["L0_attn_qkv_b"], dtype=np.float32)
    proj_w = np.asarray(inputs["L0_attn_proj_w"], dtype=np.float32)
    proj_b = np.asarray(inputs["L0_attn_proj_b"], dtype=np.float32)
    pm = np.asarray(inputs["L0_attn_projection_matrix"], dtype=np.float32)
    ctrl = np.asarray(inputs["L0_ffn_controller"], dtype=np.float32)
    f1 = np.asarray(inputs["L0_ffn_f1"], dtype=np.float32)
    f2 = np.asarray(inputs["L0_ffn_f2"], dtype=np.float32)
    fb = np.asarray(inputs["L0_ffn_bias"], dtype=np.float32)

    wqkv = np.ascontiguousarray(
        qkv_w[0:2048].reshape(16, 128, 8, 128).transpose(0, 3, 2, 1))
    wv = np.ascontiguousarray(
        qkv_w[2048:3072].reshape(8, 128, 8, 128).transpose(3, 2, 0, 1))
    qb2d = np.ascontiguousarray(qkv_b[0:2048].reshape(16, 128).T)
    pw = np.ascontiguousarray(proj_w.reshape(1024, 8, 128).transpose(2, 1, 0))
    pbt = np.ascontiguousarray(np.broadcast_to(proj_b[None, :], (128, 1024)))
    vbt = np.ascontiguousarray(np.broadcast_to(qkv_b[2048:3072][None, :], (128, 1024)))
    pm2 = np.zeros((128, 128), dtype=np.float32)
    pm2[0:64, 0:64] = pm.T
    pm2[64:128, 64:128] = pm.T
    C = ctrl.reshape(1024, 64)
    c2 = np.ascontiguousarray(
        np.concatenate([C, C], axis=1).reshape(8, 128, 128).transpose(1, 0, 2))
    f1h = np.ascontiguousarray(
        f1.reshape(1024, 64, 64).transpose(1, 0, 2)
          .reshape(64, 8, 128, 64).transpose(0, 2, 1, 3)
          .reshape(64, 128, 512))
    f2h = np.ascontiguousarray(f2.reshape(64, 64, 1024))
    fbt = np.ascontiguousarray(fb.reshape(64, 64).T)
    identb = np.eye(128, dtype=np.float32)
    mask512 = np.zeros((128, 512), dtype=np.float32)
    for g8 in range(8):
        mask512[g8*16:(g8+1)*16, g8*64:(g8+1)*64] = 1.0
    maskt = np.zeros((128, 4, 128), dtype=np.float32)
    for c in range(4):
        maskt[0:64, c, 32*c:32*c+16] = 1.0
        maskt[64:128, c, 32*c+16:32*c+32] = 1.0

    shared = dict(wqkv=wqkv, wv=wv, qb=qb2d, pw=pw, pbt=pbt, vbt=vbt, pm2=pm2,
                  c2=c2, f1h=f1h, f2h=f2h, fbt=fbt, identb=identb,
                  mask512=mask512, maskt=maskt)
    return x, shared


def kernel(**inputs):
    x, shared = _prep(inputs)
    nc = build_nc(stage=10)
    in_maps = [dict(shared, x=np.ascontiguousarray(x[c])) for c in range(8)]
    res = run_bass_kernel_spmd(nc, in_maps, core_ids=list(range(8)))
    out = np.stack([r["out"] for r in res.results], axis=0)
    return out.astype(np.float32)


# version 07
# speedup vs baseline: 1.3081x; 1.3081x over previous
import contextlib
import numpy as np
import ml_dtypes
import concourse.bass as bass
import concourse.mybir as mybir
from concourse.alu_op_type import AluOpType as OP
from concourse.bass_utils import run_bass_kernel_spmd

F32 = mybir.dt.float32
BF16 = mybir.dt.bfloat16
AF = mybir.ActivationFunctionType
AX = mybir.AxisListType

D = 1024
TOK = 2048
NTL = 16
NCH = 8
NES = 64
LN8 = float(np.log(8.0))
LNEPS = 1e-5
FEPS = 1e-6
KB = 1024
BASE = 18432

ts = bass.ts


class Sem:
    __slots__ = ("h", "n")
    def __init__(self, h):
        self.h = h
        self.n = 0
    def inc(self, inst, k=1):
        inst.then_inc(self.h, k)
        self.n += k
        return (self, self.n)
    def tok(self):
        return (self, self.n)


class Bld:
    def __init__(self, nc, blk):
        self.nc = nc
        self.blk = blk
        self.w8 = {}
    def waits(self, ename, eng, toks):
        for t in toks:
            if t is None:
                continue
            s, n = t
            if n <= 0:
                continue
            key = (ename, id(s))
            if self.w8.get(key, -1) >= n:
                continue
            eng.wait_ge(s.h, n)
            self.w8[key] = n
    def on(self, ename, toks, emitfn, inc=None, k=1):
        res = {}
        def _f(eng):
            self.waits(ename, eng, toks)
            last = emitfn(eng)
            if inc is not None:
                res["t"] = inc.inc(last, k)
        getattr(self.blk, ename)(_f)
        return res.get("t")


def build_nc(stage=10, dbg=()):
    nc = bass.Bass("TRN2", target_bir_lowering=False)

    def din(name, shape, dt=F32):
        return nc.dram_tensor(name, list(shape), dt, kind="ExternalInput").ap()

    x_h    = din("x", (TOK, D))
    wqkv_h = din("wqkv", (16, 128, 8, 128))
    wv_h   = din("wv", (128, 8, 8, 128))
    qb_h   = din("qb", (128, 16))
    pw_h   = din("pw", (128, 8, D))
    pbt_h  = din("pbt", (128, D))
    vbt_h  = din("vbt", (128, D))
    pm2_h  = din("pm2", (128, 128))
    c2_h   = din("c2", (128, 8, 128))
    f1_h   = din("f1h", (NES, 128, 512))
    f2_h   = din("f2h", (NES, 64, D))
    fbt_h  = din("fbt", (64, 64))
    id_h   = din("identb", (128, 128))
    mk5_h  = din("mask512", (128, 512))
    mkt_h  = din("maskt", (128, 4, 128))

    out_h = nc.dram_tensor("out", [TOK, D], F32, kind="ExternalOutput").ap()
    dbg_h = {}
    for name, shape in [
        ("d_ln1T", (128, 8, 2048)), ("d_ktqt", (128, 8, 2048)),
        ("d_kp", (128, 2, 1024)), ("d_v", (128, 2, 1024)),
        ("d_kvbd", (128, 8, 132)), ("d_qp", (128, 2, 1024)),
        ("d_attn", (128, 2, 1024)), ("d_ln2", (128, 16, 1024)),
        ("d_pblk", (128, 16, 512)), ("d_pblkt", (128, 16, 4, 128)),
        ("d_selt", (128, 16, 8, 512)), ("d_inner", (64, 64, 128)),
        ("d_inter", (128, 64, 512)),
    ]:
        if name in dbg:
            dbg_h[name] = nc.dram_tensor(name, list(shape), F32, kind="ExternalOutput").ap()

    A = lambda name, shape, dt, off: nc.alloc_sbuf_tensor_at(name, list(shape), dt, offset=BASE + off).ap()
    x_sb     = A("x_sb",     [128, NTL, D], F32, 0)
    selt_sb  = A("selt_sb",  [128, NTL, NCH, 512], BF16, 0)
    inter_sb = A("inter_sb", [128, NES, 512], BF16, 0)
    ln1T_sb  = A("ln1T_sb",  [128, NCH, TOK], BF16, 64 * KB)
    pwT_sb   = A("pwT_sb",   [128, NCH, D], BF16, 64 * KB)
    ktqt_sb  = A("ktqt_sb",  [128, NCH, TOK], BF16, 96 * KB)
    sq2_sb   = A("sq2_sb",   [128, D], F32, 96 * KB)
    ln2_sb   = A("ln2_sb",   [128, NTL, D], BF16, 128 * KB)
    inner_sb = A("inner_sb", [64, NES, 128], BF16, 128 * KB)
    pblk_sb  = A("pblk_sb",  [128, NTL, 512], BF16, 160 * KB)
    pblkT_sb = A("pblkT_sb", [128, NTL, 4, 128], BF16, 176 * KB)

    o = 80 * KB
    ekq_sb   = A("ekq_sb",   [128, D], BF16, o); o += 2 * KB
    qp_sb    = A("qp_sb",    [128, D], BF16, o); o += 2 * KB
    qpT_sb   = A("qpT_sb",   [128, D], BF16, o); o += 2 * KB
    attn_sb  = A("attn_sb",  [128, D], BF16, o); o += 2 * KB
    attnT_sb = A("attnT_sb", [128, D], BF16, o); o += 2 * KB
    ln2T_sb  = A("ln2T_sb",  [128, D], BF16, o); o += 2 * KB
    sq7_sb   = A("sq7_sb",   [128, D], BF16, o); o += 2 * KB
    mk5_sb   = A("mk5_sb",   [128, 512], BF16, o); o += 1 * KB
    mkt_sb   = A("mkt_sb",   [128, 4, 128], BF16, o); o += 1 * KB
    assert o <= 96 * KB

    f1big_sb = A("f1big_sb", [128, 2, 8, 512], BF16, 160 * KB)
    f2big_sb = A("f2big_sb", [64, 2, 8, 512], BF16, 144 * KB)
    stack_sb = A("stack_sb", [128, 8, 512], BF16, 96 * KB)
    x2re_sb  = A("x2re_sb",  [128, 4, 512], F32, 104 * KB)
    fin_sb   = A("fin_sb",   [128, 4, 512], F32, 112 * KB)

    o = 160 * KB
    wslab_sb = A("wslab_sb", [128, 2, NCH, 128], BF16, o); o += 4 * KB
    wv_sb    = A("wv_sb",    [128, NCH, 8, 128], BF16, o); o += 16 * KB
    vtile_sb = A("vtile_sb", [128, 2, D], BF16, o); o += 4 * KB
    kp_sb    = A("kp_sb",    [128, 2, D], BF16, o); o += 4 * KB
    ekk_sb   = A("ekk_sb",   [128, D], BF16, o); o += 2 * KB
    zro_sb   = A("zro_sb",   [128, 128], BF16, o); o += 256
    assert o <= 192 * KB

    o = 192 * KB
    def DA(name, shape, dt):
        nonlocal o
        t = A(name, shape, dt, o)
        per = int(np.prod(shape[1:])) * mybir.dt.size(dt)
        o += (per + 31) // 32 * 32
        return t
    id_sb   = DA("id_sb",   [128, 128], BF16)
    pm2_sb  = DA("pm2_sb",  [128, 128], BF16)
    c2_sb   = DA("c2_sb",   [128, NCH, 128], BF16)
    qb_sb   = DA("qb_sb",   [128, 16], F32)
    fbt_sb  = DA("fbt_sb",  [64, 64], F32)
    kvbd_sb = DA("kvbd_sb", [128, 8, 132], BF16)
    ones_sb = DA("ones_sb", [128, 2], BF16)
    st_sb   = DA("st_sb",   [128, 16], F32)
    mx_sb   = DA("mx_sb",   [128, 16], F32)
    nrm_sb  = DA("nrm_sb",  [128, 32], F32)
    gm_sb   = DA("gm_sb",   [128, 8], F32)
    pt_sb   = DA("pt_sb",   [128, 128], BF16)
    ptok_sb = DA("ptok_sb", [128, 128], BF16)
    pbt_sb  = DA("pbt_sb",  [128, D], F32)
    vbt_sb  = DA("vbt_sb",  [128, D], F32)
    eps_sb  = DA("eps_sb",  [128, 1], F32)
    ln8_sb  = DA("ln8_sb",  [128, 1], F32)
    feps_sb = DA("feps_sb", [128, 1], F32)
    mx2_sb  = DA("mx2_sb",  [128, 16], F32)
    assert o <= 206 * KB, o

    PA = nc.alloc_psum_tensor("PA", [128, 1024], F32).ap()
    PB = nc.alloc_psum_tensor("PB", [128, 1024], F32).ap()
    PC = nc.alloc_psum_tensor("PC", [128, 1024], F32).ap()
    PT1 = nc.alloc_psum_tensor("PT1", [128, 1024], BF16).ap()
    PS = nc.alloc_psum_tensor("PS", [128, 512], F32).ap()
    PQ = [PA[:, 0:512], PA[:, 512:1024], PB[:, 0:512], PB[:, 512:1024]]

    stack_cm = contextlib.ExitStack()
    sems = {}
    for nm in ["sT", "sV", "sS", "sG", "dC", "dCG", "dX", "dWV", "dPW", "dSP", "dDB",
               "dW0", "dW1", "dF0", "dF1", "dG0", "dG1",
               "dRL0", "dRL1", "dRL2", "dRL3", "dRL4", "dRL5", "dRL6", "dRL7",
               "dXR0", "dXR1", "dXR2", "dXR3", "dOU0", "dOU1", "dOU2", "dOU3"]:
        sems[nm] = Sem(stack_cm.enter_context(nc.semaphore(nm)))
    sT, sV, sS, sG = sems["sT"], sems["sV"], sems["sS"], sems["sG"]
    dC, dCG, dX, dWV, dPW, dSP, dDB = (sems[n] for n in ["dC", "dCG", "dX", "dWV", "dPW", "dSP", "dDB"])
    dW = [sems["dW0"], sems["dW1"]]
    dF = [sems["dF0"], sems["dF1"]]
    dG2 = [sems["dG0"], sems["dG1"]]
    dRL = [sems["dRL%d" % i] for i in range(8)]
    dXR = [sems["dXR%d" % i] for i in range(4)]
    dOU = [sems["dOU%d" % i] for i in range(4)]

    with nc.Block() as blk:
        b = Bld(nc, blk)

        def fin():
            toks = [s.tok() for s in sems.values()]
            b.on("sync", toks, lambda e: e.nop())

        def dbg_dump(name, src_ap, toks):
            if name in dbg_h:
                b.on("gpsimd", toks, lambda e: e.dma_start(out=dbg_h[name], in_=src_ap), dDB, 16)

        def _ld_sync(e):
            for t in range(NTL):
                e.dma_start(out=x_sb[:, t, :], in_=x_h[ts(t, 128), :]).then_inc(dX.h, 16)
            for dst, src in [(qb_sb, qb_h), (fbt_sb, fbt_h), (vbt_sb, vbt_h)]:
                e.dma_start(out=dst[:], in_=src[:]).then_inc(dC.h, 16)
            return e.dma_start(out=pbt_sb[:], in_=pbt_h[:])
        dX.n += 16 * 16
        dC.n += 3 * 16
        tok_ld = b.on("sync", [], _ld_sync, dC, 16)
        tok_x = (dX, dX.n)

        def _ld_g(e):
            e.dma_start(out=pm2_sb[:], in_=pm2_h[:]).then_inc(dCG.h, 16)
            e.dma_start(out=id_sb[:], in_=id_h[:]).then_inc(dCG.h, 16)
            return e.dma_start(out=c2_sb[:], in_=c2_h[:])
        dCG.n += 32
        tok_ldg = b.on("gpsimd", [], _ld_g, dCG, 16)
        tok_pm2 = b.on("scalar", [tok_ldg], lambda e: e.mul(pm2_sb[:], pm2_sb[:], 0.125), sS)
        def _ones(e):
            e.memset(ones_sb[:], 1.0)
            e.memset(eps_sb[:], LNEPS)
            e.memset(ln8_sb[:], LN8)
            e.memset(zro_sb[:], 0.0)
            return e.memset(feps_sb[:], FEPS)
        tok_ones = b.on("vector", [], _ones, sV)

        ln_hist = {"mu": [None, None], "var": [None, None], "ln": [None, None],
                   "n": [None, None]}
        def ln_tile(xt, out_ap, t, sq_scratch, extra=()):
            p = t % 2
            tok_sq = b.on("scalar", [ln_hist["var"][p]] + list(extra),
                lambda e: e.activation(sq_scratch[:], xt, AF.Square,
                                       accum_out=st_sb[:, p+2:p+3]), sS)
            tok_r = b.on("vector", [ln_hist["mu"][p]] + list(extra),
                lambda e: e.tensor_reduce(st_sb[:, p:p+1], xt, axis=AX.X, op=OP.add), sV)
            tok_mu = b.on("vector", [tok_r, ln_hist["n"][p]],
                lambda e: e.tensor_scalar(st_sb[:, p+4:p+5], st_sb[:, p:p+1], 1.0 / D,
                                          None, OP.mult), sV)
            ln_hist["mu"][p] = tok_mu
            tok_mus = b.on("vector", [tok_mu],
                lambda e: e.tensor_tensor(st_sb[:, p+6:p+7], st_sb[:, p+4:p+5],
                                          st_sb[:, p+4:p+5], OP.mult), sV)
            tok_var = b.on("vector", [tok_mus, tok_sq, ln_hist["ln"][p]],
                lambda e: e.tensor_scalar(st_sb[:, p+8:p+9], st_sb[:, p+2:p+3], 1.0 / D,
                                          st_sb[:, p+6:p+7], OP.mult, OP.subtract), sV)
            ln_hist["var"][p] = tok_var
            tok_ln = b.on("scalar", [tok_var, ln_hist["n"][p], tok_ones],
                lambda e: e.activation(st_sb[:, p+10:p+11], st_sb[:, p+8:p+9], AF.Ln,
                                       bias=eps_sb[:, 0:1]), sS)
            ln_hist["ln"][p] = tok_ln
            tok_rs = b.on("scalar", [tok_ln],
                lambda e: e.activation(st_sb[:, p+10:p+11], st_sb[:, p+10:p+11], AF.Exp,
                                       scale=-0.5), sS)
            tok_n = b.on("vector", [tok_rs],
                lambda e: e.tensor_scalar(out_ap, xt, st_sb[:, p+4:p+5], st_sb[:, p+10:p+11],
                                          OP.subtract, OP.mult), sV)
            ln_hist["n"][p] = tok_n
            return tok_n

        tok_tp_prev = None
        lnt_last = None
        for t in range(NTL):
            xt = x_sb[:, t, :]
            tok_n = ln_tile(xt, ekk_sb[:], t, sq2_sb, extra=[tok_x, tok_tp_prev])
            def _tt(e):
                last = None
                for c in range(NCH):
                    last = e.transpose(PT1[:, ts(c, 128)], ekk_sb[:, ts(c, 128)], id_sb[:])
                return last
            tok_tp = b.on("tensor", [tok_n, tok_ld, lnt_last], _tt, sT)
            tok_tp_prev = tok_tp
            lnt_last = b.on("vector", [tok_tp],
                lambda e, t=t: e.tensor_copy(
                    ln1T_sb[:, :, ts(t, 128)],
                    PT1[:].rearrange("p (c q) -> p c q", c=NCH)), sV)
        tok_ln1T = lnt_last
        dbg_dump("d_ln1T", ln1T_sb[:], [tok_ln1T])
        if stage < 3:
            fin(); stack_cm.close(); return nc

        def qkv_phase(jlist, barrier, seed_mm=()):
            evac_toks, mm_toks = [], list(seed_mm)
            nseed = len(mm_toks)
            for jj, j in enumerate(jlist):
                buf = jj % 2
                free_tok = mm_toks[nseed + jj - 2] if nseed + jj >= 2 else None
                tok_w = b.on("gpsimd", [free_tok, tok_ln1T],
                    lambda e, j=j, buf=buf: e.dma_start(out=wslab_sb[:, buf, :, :],
                                                        in_=wqkv_h[j, :, :, :]), dW[buf], 16)
                def _mm(e, buf=buf):
                    last = None
                    for q in range(4):
                        for c in range(NCH):
                            last = e.matmul(PQ[q], wslab_sb[:, buf, c, :],
                                            ln1T_sb[:, c, ts(q, 512)],
                                            start=(c == 0), stop=(c == NCH - 1),
                                            skip_group_check=True)
                    return last
                prev_evac = evac_toks[jj - 1] if jj >= 1 else None
                tok_mm = b.on("tensor", [tok_w, tok_ln1T, prev_evac] + barrier, _mm, sT)
                mm_toks.append(tok_mm)
                def _ev(e, jj=jj, j=j):
                    e.tensor_scalar(ktqt_sb[:, jj, 0:1024], PA[:], qb_sb[:, j:j+1], None, OP.add)
                    return e.tensor_scalar(ktqt_sb[:, jj, 1024:2048], PB[:], qb_sb[:, j:j+1],
                                           None, OP.add)
                evac_toks.append(b.on("vector", [tok_mm], _ev, sV))
            return evac_toks[-1], mm_toks

        tok_kt, k_mm_toks = qkv_phase(range(8, 16), [])
        if stage == 3:
            dbg_dump("d_ktqt", ktqt_sb[:], [tok_kt])
            fin(); stack_cm.close(); return nc

        tok_wv = b.on("gpsimd", [], lambda e: e.dma_start(out=wv_sb[:], in_=wv_h[:]), dWV, 16)
        def _open(e):
            e.matmul(PC[:, 0:512], zro_sb[:], ktqt_sb[:, 0, 0:512], start=True, stop=True,
                     skip_group_check=True)
            e.matmul(PC[:, 512:1024], zro_sb[:], ktqt_sb[:, 0, 0:512], start=True, stop=True,
                     skip_group_check=True)
            return e.matmul(PS[:, 0:8], zro_sb[:], ktqt_sb[:, 0, 0:8], start=True, stop=True,
                            skip_group_check=True)
        tok_open = b.on("tensor", [tok_kt, tok_ones], _open, sT)
        tok_sub_prev = None
        tok_vev_prev = None
        toks_mkv = [None] * NTL
        tok_kp_last = None
        tok_vt_last = None
        for t in range(NTL):
            pg = t % 2
            def _mmc(e, t=t):
                last = None
                for c in range(NCH):
                    last = e.matmul(PA[:, ts(c, 128)], ktqt_sb[:, c, ts(t, 128)], pm2_sb[:],
                                    start=True, stop=True, skip_group_check=True)
                return last
            tok_mmc = b.on("tensor", [tok_kt, tok_pm2, tok_sub_prev], _mmc, sT)
            tok_red = b.on("vector", [tok_mmc],
                lambda e: e.reduce_max(mx_sb[:], PA[:].rearrange("p (h m) -> p h m", h=16),
                                       axis=AX.X), sV)
            tok_mx2 = b.on("vector", [tok_red],
                lambda e: e.tensor_scalar(mx2_sb[:], mx_sb[:], LN8, None, OP.add), sV)
            tok_sub = b.on("vector", [tok_mx2],
                lambda e: e.scalar_tensor_tensor(
                    ekk_sb[:].rearrange("p (h m) -> p h m", h=16),
                    PA[:].rearrange("p (h m) -> p h m", h=16), 1.0,
                    mx2_sb[:].unsqueeze(2).broadcast_to([128, 16, 64]),
                    OP.mult, OP.subtract), sV)
            tok_sub_prev = tok_sub
            tok_exp = b.on("scalar", [tok_sub], lambda e: e.activation(ekk_sb[:], ekk_sb[:], AF.Exp), sS)
            tok_kp = b.on("vector", [tok_exp, toks_mkv[t - 2] if t >= 2 else None],
                lambda e, pg=pg: e.tensor_scalar(kp_sb[:, pg, :], ekk_sb[:], FEPS, None, OP.add), sV)
            tok_kp_last = tok_kp
            def _mmv(e, t=t):
                last = None
                for q in range(2):
                    for c in range(NCH):
                        last = e.matmul(PB[:, ts(q, 512)], ln1T_sb[:, c, ts(t, 128)],
                                        wv_sb[:, c, ts(q * 4, 4) if False else slice(4*q, 4*q+4), :],
                                        start=(c == 0), stop=(c == NCH - 1), skip_group_check=True)
                return last
            tok_mmv = b.on("tensor", [tok_wv, tok_vev_prev], _mmv, sT)
            tok_vt = b.on("vector", [tok_mmv, toks_mkv[t - 2] if t >= 2 else None],
                lambda e, pg=pg: e.scalar_tensor_tensor(vtile_sb[:, pg, :], PB[:], 1.0,
                                                        vbt_sb[:], OP.mult, OP.add), sV)
            tok_vev_prev = tok_vt
            tok_vt_last = tok_vt
            def _mkv(e, t=t, pg=pg):
                last = None
                for p in range(8):
                    e.matmul(PC[:, ts(p, 128)], kp_sb[:, pg, ts(p, 128)],
                             vtile_sb[:, pg, ts(p, 128)],
                             start=False, stop=(t == NTL - 1), skip_group_check=True)
                    last = e.matmul(PS[:, p:p+1], kp_sb[:, pg, ts(p, 128)], ones_sb[:, 0:1],
                                    start=False, stop=(t == NTL - 1), skip_group_check=True)
                return last
            toks_mkv[t] = b.on("tensor", [tok_kp, tok_vt, tok_open], _mkv, sT)
        tok_kv = toks_mkv[-1]
        dbg_dump("d_kp", kp_sb[:], [tok_kp_last])
        dbg_dump("d_v", vtile_sb[:], [tok_vt_last])

        def _kvbd(e):
            e.memset(kvbd_sb[:], 0.0)
            for p in range(8):
                e.tensor_copy(kvbd_sb[0:64, p, 0:64], PC[0:64, 128*p:128*p+64])
                e.tensor_copy(kvbd_sb[64:128, p, 64:128], PC[64:128, 128*p+64:128*p+128])
                e.tensor_copy(kvbd_sb[0:64, p, 128:129], PS[0:64, p:p+1])
                e.tensor_copy(kvbd_sb[64:128, p, 129:130], PS[64:128, p:p+1])
            return e.nop()
        tok_kvbd = b.on("vector", [tok_kv], _kvbd, sV)
        dbg_dump("d_kvbd", kvbd_sb[:], [tok_kvbd])
        if stage < 6:
            fin(); stack_cm.close(); return nc

        tok_qt, q_mm_toks = qkv_phase(range(0, 8), [tok_sub_prev, tok_vev_prev, tok_kvbd],
                                      seed_mm=k_mm_toks[-2:])
        def _pwmk(e):
            e.dma_start(out=mk5_sb[:], in_=mk5_h[:]).then_inc(dPW.h, 16)
            e.dma_start(out=mkt_sb[:], in_=mkt_h[:]).then_inc(dPW.h, 16)
            return e.dma_start(out=pwT_sb[:], in_=pw_h[:])
        dPW.n += 32
        tok_pw = b.on("gpsimd", [q_mm_toks[-1]], _pwmk, dPW, 16)
        if stage >= 6:
            dbg_dump("d_ktqt", ktqt_sb[:], [tok_qt])
        if stage < 7:
            fin(); stack_cm.close(); return nc

        tok_subq_prev = None
        spill_toks = [None] * NTL
        ln2_toks = [None] * NTL
        pblk_toks = [None] * NTL
        tok_mo_prev = None
        tok_attn_prev = None
        tok_mp_prev = None
        tok_x2_prev = None
        tok_mlg_prev = None
        tok_tq_prev = None
        tok_ta_prev = None
        for t in range(NTL):
            def _mmc(e, t=t):
                last = None
                for c in range(NCH):
                    last = e.matmul(PA[:, ts(c, 128)], ktqt_sb[:, c, ts(t, 128)], pm2_sb[:],
                                    start=True, stop=True, skip_group_check=True)
                return last
            tok_mmc = b.on("tensor", [tok_qt, tok_subq_prev], _mmc, sT)
            tok_redq = b.on("vector", [tok_mmc],
                lambda e: e.reduce_max(mx_sb[:], PA[:].rearrange("p (h m) -> p h m", h=16),
                                       axis=AX.X), sV)
            tok_mx2q = b.on("vector", [tok_redq],
                lambda e: e.tensor_scalar(mx2_sb[:], mx_sb[:], LN8, None, OP.add), sV)
            tok_subq = b.on("vector", [tok_mx2q],
                lambda e: e.scalar_tensor_tensor(
                    ekq_sb[:].rearrange("p (h m) -> p h m", h=16),
                    PA[:].rearrange("p (h m) -> p h m", h=16), 1.0,
                    mx2_sb[:].unsqueeze(2).broadcast_to([128, 16, 64]),
                    OP.mult, OP.subtract), sV)
            tok_subq_prev = tok_subq
            tok_expq = b.on("scalar", [tok_subq], lambda e: e.activation(ekq_sb[:], ekq_sb[:], AF.Exp), sS)
            tok_qp = b.on("vector", [tok_expq, tok_tq_prev],
                lambda e: e.tensor_scalar(qp_sb[:], ekq_sb[:], FEPS, None, OP.add), sV)
            def _tq(e):
                last = None
                for c in range(NCH):
                    last = e.transpose(PT1[:, ts(c, 128)], qp_sb[:, ts(c, 128)], id_sb[:])
                return last
            tok_tq = b.on("tensor", [tok_qp], _tq, sT)
            tok_tq_prev = tok_tq
            tok_qpT = b.on("vector", [tok_tq, tok_mo_prev],
                lambda e: e.tensor_copy(qpT_sb[:], PT1[:]), sV)
            def _mo(e):
                last = None
                for p in range(8):
                    e.matmul(PB[:, ts(p, 128)], qpT_sb[:, ts(p, 128)], kvbd_sb[:, p, 0:128],
                             start=True, stop=True, skip_group_check=True)
                    last = e.matmul(PS[:, 32+2*p:32+2*p+2], qpT_sb[:, ts(p, 128)],
                                    kvbd_sb[:, p, 128:130],
                                    start=True, stop=True, skip_group_check=True)
                return last
            tok_mo = b.on("tensor", [tok_qpT, tok_kvbd, tok_attn_prev], _mo, sT)
            tok_mo_prev = tok_mo
            tok_ne = b.on("vector", [tok_mo],
                lambda e: e.tensor_scalar(nrm_sb[:, 0:16], PS[:, 32:48], FEPS, None, OP.add), sV)
            tok_rcn = b.on("vector", [tok_ne],
                lambda e: e.reciprocal(nrm_sb[:, 16:32], nrm_sb[:, 0:16]), sV)
            tok_attn = b.on("vector", [tok_rcn, tok_ta_prev],
                lambda e: e.scalar_tensor_tensor(
                    attn_sb[:].rearrange("p (h m) -> p h m", h=16),
                    PB[:].rearrange("p (h m) -> p h m", h=16), 1.0,
                    nrm_sb[:, 16:32].unsqueeze(2).broadcast_to([128, 16, 64]),
                    OP.mult, OP.mult), sV)
            tok_attn_prev = tok_attn
            if t in (0, 1):
                if "d_qp" in dbg_h:
                    b.on("gpsimd", [tok_qp], lambda e, t=t: e.dma_start(out=dbg_h["d_qp"][:, t, :], in_=qp_sb[:]), dDB, 16)
                if "d_attn" in dbg_h:
                    b.on("gpsimd", [tok_attn], lambda e, t=t: e.dma_start(out=dbg_h["d_attn"][:, t, :], in_=attn_sb[:]), dDB, 16)
            def _ta(e):
                last = None
                for c in range(NCH):
                    last = e.transpose(PT1[:, ts(c, 128)], attn_sb[:, ts(c, 128)], id_sb[:])
                return last
            tok_ta = b.on("tensor", [tok_attn], _ta, sT)
            tok_ta_prev = tok_ta
            tok_aT = b.on("vector", [tok_ta, tok_mp_prev],
                lambda e: e.tensor_copy(attnT_sb[:], PT1[:]), sV)
            def _mp(e):
                last = None
                for q in range(2):
                    for c in range(NCH):
                        last = e.matmul(PC[:, ts(q, 512)], attnT_sb[:, ts(c, 128)],
                                        pwT_sb[:, c, ts(q, 512)],
                                        start=(c == 0), stop=(c == NCH - 1), skip_group_check=True)
                return last
            w_mp = [tok_aT, tok_pw, tok_x2_prev] + ([tok_kvbd] if t == 0 else [])
            tok_mp = b.on("tensor", w_mp, _mp, sT)
            tok_mp_prev = tok_mp
            xt = x_sb[:, t, :]
            tok_xpb = b.on("gpsimd", [tok_ld],
                lambda e, xt=xt: e.tensor_tensor(xt, xt, pbt_sb[:], OP.add), sG)
            tok_x2 = b.on("vector", [tok_mp, tok_xpb],
                lambda e, xt=xt: e.tensor_tensor(xt, PC[:], xt, OP.add), sV)
            tok_x2_prev = tok_x2
            spill_toks[t] = b.on("sync", [tok_x2],
                lambda e, xt=xt, t=t: e.dma_start(out=out_h[ts(t, 128), :], in_=xt), dSP, 16)
            ln2_toks[t] = ln_tile(xt, ln2_sb[:, t, :], t, sq7_sb, extra=[tok_x2])
            def _tl(e, t=t):
                last = None
                for c in range(NCH):
                    last = e.transpose(PT1[:, ts(c, 128)], ln2_sb[:, t, ts(c, 128)], id_sb[:])
                return last
            tok_tl = b.on("tensor", [ln2_toks[t]], _tl, sT)
            tok_l2T = b.on("vector", [tok_tl, tok_mlg_prev],
                lambda e: e.tensor_copy(ln2T_sb[:], PT1[:]), sV)
            def _ml(e):
                last = None
                for c in range(NCH):
                    last = e.matmul(PS[:, 128:256], c2_sb[:, c, :], ln2T_sb[:, ts(c, 128)],
                                    start=(c == 0), stop=(c == NCH - 1), skip_group_check=True)
                return last
            tok_mlg = b.on("tensor", [tok_l2T, tok_ldg, tok_mlg_prev], _ml, sT)
            tok_mlg_prev = tok_mlg
            tok_gm = b.on("vector", [tok_mlg],
                lambda e: e.reduce_max(gm_sb[:], PS[:, 128:256].rearrange("p (g q) -> p g q", g=8),
                                       axis=AX.X), sV)
            tok_pt = b.on("vector", [tok_gm],
                lambda e: e.tensor_tensor(pt_sb[:].rearrange("p (g q) -> p g q", g=8),
                                          PS[:, 128:256].rearrange("p (g q) -> p g q", g=8),
                                          gm_sb[:].unsqueeze(2).broadcast_to([128, 8, 16]),
                                          OP.is_equal), sV)
            tok_tpt = b.on("tensor", [tok_pt],
                lambda e: e.transpose(PT1[:, 0:128], pt_sb[:], id_sb[:]), sT)
            tok_ptok = b.on("vector", [tok_tpt],
                lambda e: e.tensor_copy(ptok_sb[:], PT1[:, 0:128]), sV)
            def _vb(e, t=t):
                e.tensor_tensor(pblk_sb[:, t, :].rearrange("p (g q) -> p g q", g=8),
                                ptok_sb[:, 0:64].unsqueeze(1).broadcast_to([128, 8, 64]),
                                mk5_sb[:].rearrange("p (g q) -> p g q", g=8), OP.mult)
                last = None
                for c in range(4):
                    last = e.tensor_tensor(pblkT_sb[:, t, c, :], pt_sb[:], mkt_sb[:, c, :],
                                           OP.mult)
                return last
            pblk_toks[t] = b.on("vector", [tok_ptok, tok_pw], _vb, sV)

        dbg_dump("d_ln2", ln2_sb[:], [ln2_toks[-1]])
        dbg_dump("d_pblk", pblk_sb[:], [pblk_toks[-1]])
        dbg_dump("d_pblkt", pblkT_sb[:], [pblk_toks[-1]])
        if stage < 8:
            fin(); stack_cm.close(); return nc

        barrier8 = [spill_toks[-1], pblk_toks[-1], sT.tok(), sV.tok(), sS.tok(), sG.tok()]
        ev_toks = []
        for t in range(NTL):
            for c in range(NCH):
                i = t * NCH + c
                q = i % 4
                free_tok = ev_toks[-4] if len(ev_toks) >= 4 else None
                tok_g = b.on("tensor", barrier8 + [pblk_toks[t], ln2_toks[t], free_tok],
                    lambda e, t=t, c=c, q=q: e.matmul(PQ[q], ln2_sb[:, t, ts(c, 128)],
                                                      pblk_sb[:, t, :], start=True, stop=True,
                                                      skip_group_check=True), sT)
                ev_toks.append(b.on("vector", [tok_g],
                    lambda e, t=t, c=c, q=q: e.tensor_copy(selt_sb[:, t, c, :], PQ[q]), sV))
        tok_selt = [sV.tok(), sS.tok()]
        dbg_dump("d_selt", selt_sb[:], tok_selt)
        if stage < 9:
            fin(); stack_cm.close(); return nc

        f1_mm_toks, relu_toks = [], []
        blk_last_mm = []
        for blkI in range(8):
            buf = blkI % 2
            free_tok = blk_last_mm[blkI - 2] if blkI >= 2 else None
            tok_w = b.on("gpsimd", [free_tok] + tok_selt,
                lambda e, blkI=blkI, buf=buf: e.dma_start(
                    out=f1big_sb[:, buf, :, :],
                    in_=f1_h[8*blkI:8*blkI+8, :, :].transpose([1, 0, 2])), dF[buf], 16)
            for e8 in range(8):
                es = 8 * blkI + e8
                psl = PA[0:64, 512 * (es % 2):512 * (es % 2) + 128]
                def _mf(e, es=es, buf=buf, e8=e8, psl=psl):
                    last = None
                    for c in range(NCH):
                        last = e.matmul(psl, f1big_sb[:, buf, e8, ts(c, 64)],
                                        selt_sb[:, :, c, es:es + 64 * 7 + 1:64],
                                        start=(c == 0), stop=(c == NCH - 1),
                                        skip_group_check=True)
                    return last
                prev_relu = relu_toks[es - 2] if es >= 2 else None
                tok_mf = b.on("tensor", [tok_w, prev_relu] + tok_selt, _mf, sT)
                f1_mm_toks.append(tok_mf)
                relu_toks.append(b.on("vector", [tok_mf],
                    lambda e, es=es, psl=psl: e.tensor_scalar(inner_sb[:, es, :], psl,
                                                              fbt_sb[:, es:es+1], 0.0,
                                                              OP.add, OP.max), sV))
            blk_last_mm.append(f1_mm_toks[-1])
        tok_inner = relu_toks[-1]
        dbg_dump("d_inner", inner_sb[:], [tok_inner])
        if stage < 10:
            fin(); stack_cm.close(); return nc

        barrier10 = [tok_inner, f1_mm_toks[-1], sT.tok()]
        fin_toks = [None] * NTL
        outdma_toks = [None] * NTL
        for h in range(2):
            f2ev_toks, f2mm_toks = [], []
            blk2_last = []
            for blkI in range(8):
                buf = blkI % 2
                free_tok = blk2_last[blkI - 2] if blkI >= 2 else None
                tok_w = b.on("gpsimd", [free_tok] + barrier10,
                    lambda e, blkI=blkI, buf=buf, h=h: e.dma_start(
                        out=f2big_sb[:, buf, :, :],
                        in_=f2_h[8*blkI:8*blkI+8, :, ts(h, 512)].transpose([1, 0, 2])), dG2[buf], 16)
                for e8 in range(8):
                    es = 8 * blkI + e8
                    q = es % 4
                    prev_ev = f2ev_toks[es - 4] if es >= 4 else None
                    tok_m2 = b.on("tensor", [tok_w, prev_ev] + barrier10,
                        lambda e, es=es, buf=buf, e8=e8, q=q: e.matmul(
                            PQ[q], inner_sb[:, es, :], f2big_sb[:, buf, e8, :], start=True,
                            stop=True, skip_group_check=True), sT)
                    f2mm_toks.append(tok_m2)
                    if es % 2 == 0:
                        f2ev_toks.append(b.on("vector", [tok_m2],
                            lambda e, es=es, q=q: e.tensor_copy(inter_sb[:, es, :], PQ[q]), sV))
                    else:
                        f2ev_toks.append(b.on("scalar", [tok_m2],
                            lambda e, es=es, q=q: e.activation(inter_sb[:, es, :], PQ[q], AF.Copy), sS))
                blk2_last.append(f2mm_toks[-1])
            tok_inter = [sV.tok(), sS.tok()]
            if h == 0:
                dbg_dump("d_inter", inter_sb[:], tok_inter)
            stk_mm_all = []
            for t in range(NTL):
                tok_x2re = b.on("scalar", [dSP.tok(), fin_toks[t - 4] if t >= 4 else None],
                    lambda e, t=t, h=h: e.dma_start(out=x2re_sb[:, t % 4, :],
                                                    in_=out_h[ts(t, 128), ts(h, 512)]), dXR[t % 4], 16)
                for c in range(4):
                    g0 = 8 * t + 2 * c
                    i = 4 * t + c
                    sbuf_i = i % 8
                    prev_use = stk_mm_all[i - 8] if i >= 8 else None
                    qeng = "sync" if c % 2 == 0 else "scalar"
                    tok_rl = b.on(qeng, tok_inter + [prev_use],
                        lambda e, g0=g0, sbuf_i=sbuf_i: e.dma_start(out=stack_sb[:, sbuf_i, :],
                                                                    in_=inter_sb[g0:g0+2, :, :]),
                        dRL[sbuf_i], 16)
                    tok_ms = b.on("tensor", [tok_rl, pblk_toks[t],
                                             fin_toks[t - 2] if t >= 2 else None],
                        lambda e, t=t, c=c, sbuf_i=sbuf_i: e.matmul(
                            PC[:, ts(t % 2, 512)], pblkT_sb[:, t, c, :],
                            stack_sb[:, sbuf_i, :], start=(c == 0),
                            stop=(c == 3), skip_group_check=True), sT)
                    stk_mm_all.append(tok_ms)
                tok_scat = stk_mm_all[-1]
                tok_fin = b.on("vector", [tok_scat, tok_x2re, outdma_toks[t - 4] if t >= 4 else None],
                    lambda e, t=t: e.tensor_tensor(fin_sb[:, t % 4, :], PC[:, ts(t % 2, 512)],
                                                   x2re_sb[:, t % 4, :], OP.add), sV)
                fin_toks[t] = tok_fin
                outdma_toks[t] = b.on("sync", [tok_fin],
                    lambda e, t=t, h=h: e.dma_start(out=out_h[ts(t, 128), ts(h, 512)],
                                                    in_=fin_sb[:, t % 4, :]), dOU[t % 4], 16)
            barrier10 = [tok_inner, sT.tok(), sV.tok(), sS.tok()] + [d.tok() for d in dRL]

        fin()
        stack_cm.close()
    return nc


def _prep(inputs):
    x = np.asarray(inputs["L0_attn_x"], dtype=np.float32)
    qkv_w = np.ascontiguousarray(np.asarray(inputs["L0_attn_qkv_w"], dtype=np.float32))
    qkv_b = np.asarray(inputs["L0_attn_qkv_b"], dtype=np.float32)
    proj_w = np.asarray(inputs["L0_attn_proj_w"], dtype=np.float32)
    proj_b = np.asarray(inputs["L0_attn_proj_b"], dtype=np.float32)
    pm = np.asarray(inputs["L0_attn_projection_matrix"], dtype=np.float32)
    ctrl = np.asarray(inputs["L0_ffn_controller"], dtype=np.float32)
    f1 = np.asarray(inputs["L0_ffn_f1"], dtype=np.float32)
    f2 = np.asarray(inputs["L0_ffn_f2"], dtype=np.float32)
    fb = np.asarray(inputs["L0_ffn_bias"], dtype=np.float32)

    wqkv = np.ascontiguousarray(
        qkv_w[0:2048].reshape(16, 128, 8, 128).transpose(0, 3, 2, 1))
    wv = np.ascontiguousarray(
        qkv_w[2048:3072].reshape(8, 128, 8, 128).transpose(3, 2, 0, 1))
    qb2d = np.ascontiguousarray(qkv_b[0:2048].reshape(16, 128).T)
    pw = np.ascontiguousarray(proj_w.reshape(1024, 8, 128).transpose(2, 1, 0))
    pbt = np.ascontiguousarray(np.broadcast_to(proj_b[None, :], (128, 1024)))
    vbt = np.ascontiguousarray(np.broadcast_to(qkv_b[2048:3072][None, :], (128, 1024)))
    pm2 = np.zeros((128, 128), dtype=np.float32)
    pm2[0:64, 0:64] = pm.T
    pm2[64:128, 64:128] = pm.T
    C = ctrl.reshape(1024, 64)
    c2 = np.ascontiguousarray(
        np.concatenate([C, C], axis=1).reshape(8, 128, 128).transpose(1, 0, 2))
    f1h = np.ascontiguousarray(
        f1.reshape(1024, 64, 64).transpose(1, 0, 2)
          .reshape(64, 8, 128, 64).transpose(0, 2, 1, 3)
          .reshape(64, 128, 512))
    f2h = np.ascontiguousarray(f2.reshape(64, 64, 1024))
    fbt = np.ascontiguousarray(fb.reshape(64, 64).T)
    identb = np.eye(128, dtype=np.float32)
    mask512 = np.zeros((128, 512), dtype=np.float32)
    for g8 in range(8):
        mask512[g8*16:(g8+1)*16, g8*64:(g8+1)*64] = 1.0
    maskt = np.zeros((128, 4, 128), dtype=np.float32)
    for c in range(4):
        maskt[0:64, c, 32*c:32*c+16] = 1.0
        maskt[64:128, c, 32*c+16:32*c+32] = 1.0

    shared = dict(wqkv=wqkv, wv=wv, qb=qb2d, pw=pw, pbt=pbt, vbt=vbt, pm2=pm2,
                  c2=c2, f1h=f1h, f2h=f2h, fbt=fbt, identb=identb,
                  mask512=mask512, maskt=maskt)
    return x, shared


def kernel(**inputs):
    x, shared = _prep(inputs)
    nc = build_nc(stage=10)
    in_maps = [dict(shared, x=np.ascontiguousarray(x[c])) for c in range(8)]
    res = run_bass_kernel_spmd(nc, in_maps, core_ids=list(range(8)))
    out = np.stack([r["out"] for r in res.results], axis=0)
    return out.astype(np.float32)


# version 08
# speedup vs baseline: 1.3155x; 1.0056x over previous
import contextlib
import numpy as np
import ml_dtypes
import concourse.bass as bass
import concourse.mybir as mybir
from concourse.alu_op_type import AluOpType as OP
from concourse.bass_utils import run_bass_kernel_spmd

F32 = mybir.dt.float32
BF16 = mybir.dt.bfloat16
AF = mybir.ActivationFunctionType
AX = mybir.AxisListType

D = 1024
TOK = 2048
NTL = 16
NCH = 8
NES = 64
LN8 = float(np.log(8.0))
LNEPS = 1e-5
FEPS = 1e-6
KB = 1024
BASE = 18432

ts = bass.ts


class Sem:
    __slots__ = ("h", "n")
    def __init__(self, h):
        self.h = h
        self.n = 0
    def inc(self, inst, k=1):
        inst.then_inc(self.h, k)
        self.n += k
        return (self, self.n)
    def tok(self):
        return (self, self.n)


class Bld:
    def __init__(self, nc, blk):
        self.nc = nc
        self.blk = blk
        self.w8 = {}
    def waits(self, ename, eng, toks):
        for t in toks:
            if t is None:
                continue
            s, n = t
            if n <= 0:
                continue
            key = (ename, id(s))
            if self.w8.get(key, -1) >= n:
                continue
            eng.wait_ge(s.h, n)
            self.w8[key] = n
    def on(self, ename, toks, emitfn, inc=None, k=1):
        res = {}
        def _f(eng):
            self.waits(ename, eng, toks)
            last = emitfn(eng)
            if inc is not None:
                res["t"] = inc.inc(last, k)
        getattr(self.blk, ename)(_f)
        return res.get("t")


def build_nc(stage=10, dbg=()):
    nc = bass.Bass("TRN2", target_bir_lowering=False)

    def din(name, shape, dt=F32):
        return nc.dram_tensor(name, list(shape), dt, kind="ExternalInput").ap()

    x_h    = din("x", (TOK, D))
    wqkv_h = din("wqkv", (16, 128, 8, 128))
    wv_h   = din("wv", (128, 8, 8, 128))
    qb_h   = din("qb", (128, 16))
    pw_h   = din("pw", (128, 8, D))
    pbt_h  = din("pbt", (128, D))
    vbt_h  = din("vbt", (128, D))
    pm2_h  = din("pm2", (128, 128))
    c2_h   = din("c2", (128, 8, 128))
    f1_h   = din("f1h", (NES, 128, 512))
    f2_h   = din("f2h", (NES, 64, D))
    fbt_h  = din("fbt", (64, 64))
    id_h   = din("identb", (128, 128))
    mk5_h  = din("mask512", (128, 512))
    mkt_h  = din("maskt", (128, 4, 128))

    out_h = nc.dram_tensor("out", [TOK, D], F32, kind="ExternalOutput").ap()
    dbg_h = {}
    for name, shape in [
        ("d_ln1T", (128, 8, 2048)), ("d_ktqt", (128, 8, 2048)),
        ("d_kp", (128, 2, 1024)), ("d_v", (128, 2, 1024)),
        ("d_kvbd", (128, 8, 132)), ("d_qp", (128, 2, 1024)),
        ("d_attn", (128, 2, 1024)), ("d_ln2", (128, 16, 1024)),
        ("d_pblk", (128, 16, 512)), ("d_pblkt", (128, 16, 4, 128)),
        ("d_selt", (128, 16, 8, 512)), ("d_inner", (64, 64, 128)),
        ("d_inter", (128, 64, 512)),
    ]:
        if name in dbg:
            dbg_h[name] = nc.dram_tensor(name, list(shape), F32, kind="ExternalOutput").ap()

    A = lambda name, shape, dt, off: nc.alloc_sbuf_tensor_at(name, list(shape), dt, offset=BASE + off).ap()
    x_sb     = A("x_sb",     [128, NTL, D], F32, 0)
    selt_sb  = A("selt_sb",  [128, NTL, NCH, 512], BF16, 0)
    inter_sb = A("inter_sb", [128, NES, 512], BF16, 0)
    ln1T_sb  = A("ln1T_sb",  [128, NCH, TOK], BF16, 64 * KB)
    pwT_sb   = A("pwT_sb",   [128, NCH, D], BF16, 64 * KB)
    ktqt_sb  = A("ktqt_sb",  [128, NCH, TOK], BF16, 96 * KB)
    sq2_sb   = A("sq2_sb",   [128, D], F32, 96 * KB)
    ln2_sb   = A("ln2_sb",   [128, NTL, D], BF16, 128 * KB)
    inner_sb = A("inner_sb", [64, NES, 128], BF16, 128 * KB)
    pblk_sb  = A("pblk_sb",  [128, NTL, 512], BF16, 160 * KB)
    pblkT_sb = A("pblkT_sb", [128, NTL, 4, 128], BF16, 176 * KB)

    o = 80 * KB
    ekq_sb   = A("ekq_sb",   [128, D], BF16, o); o += 2 * KB
    qp_sb    = A("qp_sb",    [128, D], BF16, o); o += 2 * KB
    qpT_sb   = A("qpT_sb",   [128, D], BF16, o); o += 2 * KB
    attn_sb  = A("attn_sb",  [128, D], BF16, o); o += 2 * KB
    attnT_sb = A("attnT_sb", [128, D], BF16, o); o += 2 * KB
    ln2T_sb  = A("ln2T_sb",  [128, D], BF16, o); o += 2 * KB
    sq7_sb   = A("sq7_sb",   [128, D], BF16, o); o += 2 * KB
    mk5_sb   = A("mk5_sb",   [128, 512], BF16, o); o += 1 * KB
    mkt_sb   = A("mkt_sb",   [128, 4, 128], BF16, o); o += 1 * KB
    assert o <= 96 * KB

    f1big_sb = A("f1big_sb", [128, 2, 8, 512], BF16, 160 * KB)
    f2big_sb = A("f2big_sb", [64, 2, 8, 512], BF16, 144 * KB)
    stack_sb = A("stack_sb", [128, 8, 512], BF16, 96 * KB)
    x2re_sb  = A("x2re_sb",  [128, 4, 512], F32, 104 * KB)
    fin_sb   = A("fin_sb",   [128, 4, 512], F32, 112 * KB)

    o = 160 * KB
    wslab_sb = A("wslab_sb", [128, 2, NCH, 128], BF16, o); o += 4 * KB
    wv_sb    = A("wv_sb",    [128, NCH, 8, 128], BF16, o); o += 16 * KB
    vtile_sb = A("vtile_sb", [128, 2, D], BF16, o); o += 4 * KB
    kp_sb    = A("kp_sb",    [128, 2, D], BF16, o); o += 4 * KB
    ekk_sb   = A("ekk_sb",   [128, D], BF16, o); o += 2 * KB
    zro_sb   = A("zro_sb",   [128, 128], BF16, o); o += 256
    assert o <= 192 * KB

    o = 192 * KB
    def DA(name, shape, dt):
        nonlocal o
        t = A(name, shape, dt, o)
        per = int(np.prod(shape[1:])) * mybir.dt.size(dt)
        o += (per + 31) // 32 * 32
        return t
    id_sb   = DA("id_sb",   [128, 128], BF16)
    pm2_sb  = DA("pm2_sb",  [128, 128], BF16)
    c2_sb   = DA("c2_sb",   [128, NCH, 128], BF16)
    qb_sb   = DA("qb_sb",   [128, 16], F32)
    fbt_sb  = DA("fbt_sb",  [64, 64], F32)
    kvbd_sb = DA("kvbd_sb", [128, 8, 132], BF16)
    ones_sb = DA("ones_sb", [128, 2], BF16)
    st_sb   = DA("st_sb",   [128, 16], F32)
    mx_sb   = DA("mx_sb",   [128, 16], F32)
    nrm_sb  = DA("nrm_sb",  [128, 32], F32)
    gm_sb   = DA("gm_sb",   [128, 8], F32)
    pt_sb   = DA("pt_sb",   [128, 128], BF16)
    ptok_sb = DA("ptok_sb", [128, 128], BF16)
    pbt_sb  = DA("pbt_sb",  [128, D], F32)
    vbt_sb  = DA("vbt_sb",  [128, D], F32)
    eps_sb  = DA("eps_sb",  [128, 1], F32)
    ln8_sb  = DA("ln8_sb",  [128, 1], F32)
    feps_sb = DA("feps_sb", [128, 1], F32)
    mx2_sb  = DA("mx2_sb",  [128, 16], F32)
    assert o <= 206 * KB, o

    PA = nc.alloc_psum_tensor("PA", [128, 1024], F32).ap()
    PB = nc.alloc_psum_tensor("PB", [128, 1024], F32).ap()
    PC = nc.alloc_psum_tensor("PC", [128, 1024], F32).ap()
    PT1 = nc.alloc_psum_tensor("PT1", [128, 1024], BF16).ap()
    PS = nc.alloc_psum_tensor("PS", [128, 512], F32).ap()
    PQ = [PA[:, 0:512], PA[:, 512:1024], PB[:, 0:512], PB[:, 512:1024]]

    stack_cm = contextlib.ExitStack()
    sems = {}
    for nm in ["sT", "sV", "sS", "sG", "dC", "dCG", "dX", "dWV", "dPW", "dSP", "dDB",
               "dW0", "dW1", "dF0", "dF1", "dG0", "dG1",
               "dRL0", "dRL1", "dRL2", "dRL3", "dRL4", "dRL5", "dRL6", "dRL7",
               "dXR0", "dXR1", "dXR2", "dXR3", "dOU0", "dOU1", "dOU2", "dOU3"]:
        sems[nm] = Sem(stack_cm.enter_context(nc.semaphore(nm)))
    sT, sV, sS, sG = sems["sT"], sems["sV"], sems["sS"], sems["sG"]
    dC, dCG, dX, dWV, dPW, dSP, dDB = (sems[n] for n in ["dC", "dCG", "dX", "dWV", "dPW", "dSP", "dDB"])
    dW = [sems["dW0"], sems["dW1"]]
    dF = [sems["dF0"], sems["dF1"]]
    dG2 = [sems["dG0"], sems["dG1"]]
    dRL = [sems["dRL%d" % i] for i in range(8)]
    dXR = [sems["dXR%d" % i] for i in range(4)]
    dOU = [sems["dOU%d" % i] for i in range(4)]

    with nc.Block() as blk:
        b = Bld(nc, blk)

        def fin():
            toks = [s.tok() for s in sems.values()]
            b.on("sync", toks, lambda e: e.nop())

        def dbg_dump(name, src_ap, toks):
            if name in dbg_h:
                b.on("gpsimd", toks, lambda e: e.dma_start(out=dbg_h[name], in_=src_ap), dDB, 16)

        def _ld_sync(e):
            for t in range(NTL):
                e.dma_start(out=x_sb[:, t, :], in_=x_h[ts(t, 128), :]).then_inc(dX.h, 16)
            for dst, src in [(qb_sb, qb_h), (fbt_sb, fbt_h), (vbt_sb, vbt_h)]:
                e.dma_start(out=dst[:], in_=src[:]).then_inc(dC.h, 16)
            return e.dma_start(out=pbt_sb[:], in_=pbt_h[:])
        dX.n += 16 * 16
        dC.n += 3 * 16
        tok_ld = b.on("sync", [], _ld_sync, dC, 16)
        tok_x = (dX, dX.n)

        def _ld_g(e):
            e.dma_start(out=pm2_sb[:], in_=pm2_h[:]).then_inc(dCG.h, 16)
            e.dma_start(out=id_sb[:], in_=id_h[:]).then_inc(dCG.h, 16)
            return e.dma_start(out=c2_sb[:], in_=c2_h[:])
        dCG.n += 32
        tok_ldg = b.on("gpsimd", [], _ld_g, dCG, 16)
        tok_pm2 = b.on("scalar", [tok_ldg], lambda e: e.mul(pm2_sb[:], pm2_sb[:], 0.125), sS)
        def _ones(e):
            e.memset(ones_sb[:], 1.0)
            e.memset(eps_sb[:], LNEPS)
            e.memset(ln8_sb[:], LN8)
            e.memset(zro_sb[:], 0.0)
            return e.memset(feps_sb[:], FEPS)
        tok_ones = b.on("vector", [], _ones, sV)

        ln_hist = {"mu": [None, None], "var": [None, None], "ln": [None, None],
                   "n": [None, None]}
        def ln_tile(xt, out_ap, t, sq_scratch, extra=()):
            p = t % 2
            tok_sq = b.on("scalar", [ln_hist["var"][p]] + list(extra),
                lambda e: e.activation(sq_scratch[:], xt, AF.Square,
                                       accum_out=st_sb[:, p+2:p+3]), sS)
            tok_r = b.on("vector", [ln_hist["mu"][p]] + list(extra),
                lambda e: e.tensor_reduce(st_sb[:, p:p+1], xt, axis=AX.X, op=OP.add), sV)
            tok_mu = b.on("vector", [tok_r, ln_hist["n"][p]],
                lambda e: e.tensor_scalar(st_sb[:, p+4:p+5], st_sb[:, p:p+1], 1.0 / D,
                                          None, OP.mult), sV)
            ln_hist["mu"][p] = tok_mu
            tok_mus = b.on("vector", [tok_mu],
                lambda e: e.tensor_tensor(st_sb[:, p+6:p+7], st_sb[:, p+4:p+5],
                                          st_sb[:, p+4:p+5], OP.mult), sV)
            tok_var = b.on("vector", [tok_mus, tok_sq, ln_hist["ln"][p]],
                lambda e: e.tensor_scalar(st_sb[:, p+8:p+9], st_sb[:, p+2:p+3], 1.0 / D,
                                          st_sb[:, p+6:p+7], OP.mult, OP.subtract), sV)
            ln_hist["var"][p] = tok_var
            tok_ln = b.on("scalar", [tok_var, ln_hist["n"][p], tok_ones],
                lambda e: e.activation(st_sb[:, p+10:p+11], st_sb[:, p+8:p+9], AF.Ln,
                                       bias=eps_sb[:, 0:1]), sS)
            ln_hist["ln"][p] = tok_ln
            tok_rs = b.on("scalar", [tok_ln],
                lambda e: e.activation(st_sb[:, p+10:p+11], st_sb[:, p+10:p+11], AF.Exp,
                                       scale=-0.5), sS)
            tok_n = b.on("vector", [tok_rs],
                lambda e: e.tensor_scalar(out_ap, xt, st_sb[:, p+4:p+5], st_sb[:, p+10:p+11],
                                          OP.subtract, OP.mult), sV)
            ln_hist["n"][p] = tok_n
            return tok_n

        tok_tp_prev = None
        lnt_last = None
        for t in range(NTL):
            xt = x_sb[:, t, :]
            tok_n = ln_tile(xt, ekk_sb[:], t, sq2_sb, extra=[tok_x, tok_tp_prev])
            def _tt(e):
                last = None
                for c in range(NCH):
                    last = e.transpose(PT1[:, ts(c, 128)], ekk_sb[:, ts(c, 128)], id_sb[:])
                return last
            tok_tp = b.on("tensor", [tok_n, tok_ld, lnt_last], _tt, sT)
            tok_tp_prev = tok_tp
            lnt_last = b.on("vector", [tok_tp],
                lambda e, t=t: e.tensor_copy(
                    ln1T_sb[:, :, ts(t, 128)],
                    PT1[:].rearrange("p (c q) -> p c q", c=NCH)), sV)
        tok_ln1T = lnt_last
        dbg_dump("d_ln1T", ln1T_sb[:], [tok_ln1T])
        if stage < 3:
            fin(); stack_cm.close(); return nc

        def qkv_phase(jlist, barrier, seed_mm=()):
            evac_toks, mm_toks = [], list(seed_mm)
            nseed = len(mm_toks)
            for jj, j in enumerate(jlist):
                buf = jj % 2
                free_tok = mm_toks[nseed + jj - 2] if nseed + jj >= 2 else None
                tok_w = b.on("gpsimd", [free_tok, tok_ln1T],
                    lambda e, j=j, buf=buf: e.dma_start(out=wslab_sb[:, buf, :, :],
                                                        in_=wqkv_h[j, :, :, :]), dW[buf], 16)
                def _mm(e, buf=buf):
                    last = None
                    for q in range(4):
                        for c in range(NCH):
                            last = e.matmul(PQ[q], wslab_sb[:, buf, c, :],
                                            ln1T_sb[:, c, ts(q, 512)],
                                            start=(c == 0), stop=(c == NCH - 1),
                                            skip_group_check=True)
                    return last
                prev_evac = evac_toks[jj - 1] if jj >= 1 else None
                tok_mm = b.on("tensor", [tok_w, tok_ln1T, prev_evac] + barrier, _mm, sT)
                mm_toks.append(tok_mm)
                def _ev(e, jj=jj, j=j):
                    e.tensor_scalar(ktqt_sb[:, jj, 0:1024], PA[:], qb_sb[:, j:j+1], None, OP.add)
                    return e.tensor_scalar(ktqt_sb[:, jj, 1024:2048], PB[:], qb_sb[:, j:j+1],
                                           None, OP.add)
                evac_toks.append(b.on("vector", [tok_mm], _ev, sV))
            return evac_toks[-1], mm_toks

        tok_kt, k_mm_toks = qkv_phase(range(8, 16), [])
        if stage == 3:
            dbg_dump("d_ktqt", ktqt_sb[:], [tok_kt])
            fin(); stack_cm.close(); return nc

        tok_wv = b.on("gpsimd", [], lambda e: e.dma_start(out=wv_sb[:], in_=wv_h[:]), dWV, 16)
        def _open(e):
            e.matmul(PC[:, 0:512], zro_sb[:], ktqt_sb[:, 0, 0:512], start=True, stop=True,
                     skip_group_check=True)
            e.matmul(PC[:, 512:1024], zro_sb[:], ktqt_sb[:, 0, 0:512], start=True, stop=True,
                     skip_group_check=True)
            return e.matmul(PS[:, 0:8], zro_sb[:], ktqt_sb[:, 0, 0:8], start=True, stop=True,
                            skip_group_check=True)
        tok_open = b.on("tensor", [tok_kt, tok_ones], _open, sT)
        tok_sub_prev = None
        tok_vev_prev = None
        toks_mkv = [None] * NTL
        tok_kp_last = None
        tok_vt_last = None
        for t in range(NTL):
            pg = t % 2
            def _mmc(e, t=t):
                last = None
                for c in range(NCH):
                    last = e.matmul(PA[:, ts(c, 128)], ktqt_sb[:, c, ts(t, 128)], pm2_sb[:],
                                    start=True, stop=True, skip_group_check=True)
                return last
            tok_mmc = b.on("tensor", [tok_kt, tok_pm2, tok_sub_prev], _mmc, sT)
            tok_red = b.on("vector", [tok_mmc],
                lambda e: e.reduce_max(mx_sb[:], PA[:].rearrange("p (h m) -> p h m", h=16),
                                       axis=AX.X), sV)
            tok_mx2 = b.on("vector", [tok_red],
                lambda e: e.tensor_scalar(mx2_sb[:], mx_sb[:], LN8, None, OP.add), sV)
            tok_sub = b.on("vector", [tok_mx2],
                lambda e: e.scalar_tensor_tensor(
                    ekk_sb[:].rearrange("p (h m) -> p h m", h=16),
                    PA[:].rearrange("p (h m) -> p h m", h=16), 1.0,
                    mx2_sb[:].unsqueeze(2).broadcast_to([128, 16, 64]),
                    OP.mult, OP.subtract), sV)
            tok_sub_prev = tok_sub
            tok_exp = b.on("scalar", [tok_sub], lambda e: e.activation(ekk_sb[:], ekk_sb[:], AF.Exp), sS)
            tok_kp = b.on("vector", [tok_exp, toks_mkv[t - 2] if t >= 2 else None],
                lambda e, pg=pg: e.tensor_scalar(kp_sb[:, pg, :], ekk_sb[:], FEPS, None, OP.add), sV)
            tok_kp_last = tok_kp
            def _mmv(e, t=t):
                last = None
                for q in range(2):
                    for c in range(NCH):
                        last = e.matmul(PB[:, ts(q, 512)], ln1T_sb[:, c, ts(t, 128)],
                                        wv_sb[:, c, ts(q * 4, 4) if False else slice(4*q, 4*q+4), :],
                                        start=(c == 0), stop=(c == NCH - 1), skip_group_check=True)
                return last
            tok_mmv = b.on("tensor", [tok_wv, tok_vev_prev], _mmv, sT)
            tok_vt = b.on("vector", [tok_mmv, toks_mkv[t - 2] if t >= 2 else None],
                lambda e, pg=pg: e.scalar_tensor_tensor(vtile_sb[:, pg, :], PB[:], 1.0,
                                                        vbt_sb[:], OP.mult, OP.add), sV)
            tok_vev_prev = tok_vt
            tok_vt_last = tok_vt
            def _mkv(e, t=t, pg=pg):
                last = None
                for p in range(8):
                    e.matmul(PC[:, ts(p, 128)], kp_sb[:, pg, ts(p, 128)],
                             vtile_sb[:, pg, ts(p, 128)],
                             start=False, stop=(t == NTL - 1), skip_group_check=True)
                    last = e.matmul(PS[:, p:p+1], kp_sb[:, pg, ts(p, 128)], ones_sb[:, 0:1],
                                    start=False, stop=(t == NTL - 1), skip_group_check=True)
                return last
            toks_mkv[t] = b.on("tensor", [tok_kp, tok_vt, tok_open], _mkv, sT)
        tok_kv = toks_mkv[-1]
        dbg_dump("d_kp", kp_sb[:], [tok_kp_last])
        dbg_dump("d_v", vtile_sb[:], [tok_vt_last])

        def _kvbd(e):
            e.memset(kvbd_sb[:], 0.0)
            for p in range(8):
                e.tensor_copy(kvbd_sb[0:64, p, 0:64], PC[0:64, 128*p:128*p+64])
                e.tensor_copy(kvbd_sb[64:128, p, 64:128], PC[64:128, 128*p+64:128*p+128])
                e.tensor_copy(kvbd_sb[0:64, p, 128:129], PS[0:64, p:p+1])
                e.tensor_copy(kvbd_sb[64:128, p, 129:130], PS[64:128, p:p+1])
            return e.nop()
        tok_kvbd = b.on("vector", [tok_kv], _kvbd, sV)
        dbg_dump("d_kvbd", kvbd_sb[:], [tok_kvbd])
        if stage < 6:
            fin(); stack_cm.close(); return nc

        tok_qt, q_mm_toks = qkv_phase(range(0, 8), [tok_sub_prev, tok_vev_prev, tok_kvbd],
                                      seed_mm=k_mm_toks[-2:])
        def _pwmk(e):
            e.dma_start(out=mk5_sb[:], in_=mk5_h[:]).then_inc(dPW.h, 16)
            e.dma_start(out=mkt_sb[:], in_=mkt_h[:]).then_inc(dPW.h, 16)
            return e.dma_start(out=pwT_sb[:], in_=pw_h[:])
        dPW.n += 32
        tok_pw = b.on("gpsimd", [q_mm_toks[-1]], _pwmk, dPW, 16)
        if stage >= 6:
            dbg_dump("d_ktqt", ktqt_sb[:], [tok_qt])
        if stage < 7:
            fin(); stack_cm.close(); return nc

        tok_subq_prev = None
        spill_toks = [None] * NTL
        ln2_toks = [None] * NTL
        pblk_toks = [None] * NTL
        tok_mo_prev = None
        tok_attn_prev = None
        tok_mp_prev = None
        tok_x2_prev = None
        tok_mlg_prev = None
        tok_tq_prev = None
        tok_ta_prev = None
        for t in range(NTL):
            def _mmc(e, t=t):
                last = None
                for c in range(NCH):
                    last = e.matmul(PA[:, ts(c, 128)], ktqt_sb[:, c, ts(t, 128)], pm2_sb[:],
                                    start=True, stop=True, skip_group_check=True)
                return last
            tok_mmc = b.on("tensor", [tok_qt, tok_subq_prev], _mmc, sT)
            tok_redq = b.on("vector", [tok_mmc],
                lambda e: e.reduce_max(mx_sb[:], PA[:].rearrange("p (h m) -> p h m", h=16),
                                       axis=AX.X), sV)
            tok_mx2q = b.on("vector", [tok_redq],
                lambda e: e.tensor_scalar(mx2_sb[:], mx_sb[:], LN8, None, OP.add), sV)
            tok_subq = b.on("vector", [tok_mx2q],
                lambda e: e.scalar_tensor_tensor(
                    ekq_sb[:].rearrange("p (h m) -> p h m", h=16),
                    PA[:].rearrange("p (h m) -> p h m", h=16), 1.0,
                    mx2_sb[:].unsqueeze(2).broadcast_to([128, 16, 64]),
                    OP.mult, OP.subtract), sV)
            tok_subq_prev = tok_subq
            tok_expq = b.on("scalar", [tok_subq], lambda e: e.activation(ekq_sb[:], ekq_sb[:], AF.Exp), sS)
            tok_qp = b.on("vector", [tok_expq, tok_tq_prev],
                lambda e: e.tensor_scalar(qp_sb[:], ekq_sb[:], FEPS, None, OP.add), sV)
            def _tq(e):
                last = None
                for c in range(NCH):
                    last = e.transpose(PT1[:, ts(c, 128)], qp_sb[:, ts(c, 128)], id_sb[:])
                return last
            tok_tq = b.on("tensor", [tok_qp], _tq, sT)
            tok_tq_prev = tok_tq
            tok_qpT = b.on("vector", [tok_tq, tok_mo_prev],
                lambda e: e.tensor_copy(qpT_sb[:], PT1[:]), sV)
            def _mo(e):
                last = None
                for p in range(8):
                    e.matmul(PB[:, ts(p, 128)], qpT_sb[:, ts(p, 128)], kvbd_sb[:, p, 0:128],
                             start=True, stop=True, skip_group_check=True)
                    last = e.matmul(PS[:, 32+2*p:32+2*p+2], qpT_sb[:, ts(p, 128)],
                                    kvbd_sb[:, p, 128:130],
                                    start=True, stop=True, skip_group_check=True)
                return last
            tok_mo = b.on("tensor", [tok_qpT, tok_kvbd, tok_attn_prev], _mo, sT)
            tok_mo_prev = tok_mo
            tok_ne = b.on("vector", [tok_mo],
                lambda e: e.tensor_scalar(nrm_sb[:, 0:16], PS[:, 32:48], FEPS, None, OP.add), sV)
            tok_rcn = b.on("vector", [tok_ne],
                lambda e: e.reciprocal(nrm_sb[:, 16:32], nrm_sb[:, 0:16]), sV)
            tok_attn = b.on("vector", [tok_rcn, tok_ta_prev],
                lambda e: e.scalar_tensor_tensor(
                    attn_sb[:].rearrange("p (h m) -> p h m", h=16),
                    PB[:].rearrange("p (h m) -> p h m", h=16), 1.0,
                    nrm_sb[:, 16:32].unsqueeze(2).broadcast_to([128, 16, 64]),
                    OP.mult, OP.mult), sV)
            tok_attn_prev = tok_attn
            if t in (0, 1):
                if "d_qp" in dbg_h:
                    b.on("gpsimd", [tok_qp], lambda e, t=t: e.dma_start(out=dbg_h["d_qp"][:, t, :], in_=qp_sb[:]), dDB, 16)
                if "d_attn" in dbg_h:
                    b.on("gpsimd", [tok_attn], lambda e, t=t: e.dma_start(out=dbg_h["d_attn"][:, t, :], in_=attn_sb[:]), dDB, 16)
            def _ta(e):
                last = None
                for c in range(NCH):
                    last = e.transpose(PT1[:, ts(c, 128)], attn_sb[:, ts(c, 128)], id_sb[:])
                return last
            tok_ta = b.on("tensor", [tok_attn], _ta, sT)
            tok_ta_prev = tok_ta
            tok_aT = b.on("vector", [tok_ta, tok_mp_prev],
                lambda e: e.tensor_copy(attnT_sb[:], PT1[:]), sV)
            def _mp(e):
                last = None
                for q in range(2):
                    for c in range(NCH):
                        last = e.matmul(PC[:, ts(q, 512)], attnT_sb[:, ts(c, 128)],
                                        pwT_sb[:, c, ts(q, 512)],
                                        start=(c == 0), stop=(c == NCH - 1), skip_group_check=True)
                return last
            w_mp = [tok_aT, tok_pw, tok_x2_prev] + ([tok_kvbd] if t == 0 else [])
            tok_mp = b.on("tensor", w_mp, _mp, sT)
            tok_mp_prev = tok_mp
            xt = x_sb[:, t, :]
            tok_xpb = b.on("gpsimd", [tok_ld],
                lambda e, xt=xt: e.tensor_tensor(xt, xt, pbt_sb[:], OP.add), sG)
            tok_x2 = b.on("vector", [tok_mp, tok_xpb],
                lambda e, xt=xt: e.tensor_tensor(xt, PC[:], xt, OP.add), sV)
            tok_x2_prev = tok_x2
            spill_toks[t] = b.on("sync", [tok_x2],
                lambda e, xt=xt, t=t: e.dma_start(out=out_h[ts(t, 128), :], in_=xt), dSP, 16)
            ln2_toks[t] = ln_tile(xt, ln2_sb[:, t, :], t, sq7_sb, extra=[tok_x2])
            def _tl(e, t=t):
                last = None
                for c in range(NCH):
                    last = e.transpose(PT1[:, ts(c, 128)], ln2_sb[:, t, ts(c, 128)], id_sb[:])
                return last
            tok_tl = b.on("tensor", [ln2_toks[t]], _tl, sT)
            tok_l2T = b.on("vector", [tok_tl, tok_mlg_prev],
                lambda e: e.tensor_copy(ln2T_sb[:], PT1[:]), sV)
            def _ml(e):
                last = None
                for c in range(NCH):
                    last = e.matmul(PS[:, 128:256], c2_sb[:, c, :], ln2T_sb[:, ts(c, 128)],
                                    start=(c == 0), stop=(c == NCH - 1), skip_group_check=True)
                return last
            tok_mlg = b.on("tensor", [tok_l2T, tok_ldg, tok_mlg_prev], _ml, sT)
            tok_mlg_prev = tok_mlg
            tok_gm = b.on("vector", [tok_mlg],
                lambda e: e.reduce_max(gm_sb[:], PS[:, 128:256].rearrange("p (g q) -> p g q", g=8),
                                       axis=AX.X), sV)
            tok_pt = b.on("vector", [tok_gm],
                lambda e: e.tensor_tensor(pt_sb[:].rearrange("p (g q) -> p g q", g=8),
                                          PS[:, 128:256].rearrange("p (g q) -> p g q", g=8),
                                          gm_sb[:].unsqueeze(2).broadcast_to([128, 8, 16]),
                                          OP.is_equal), sV)
            tok_tpt = b.on("tensor", [tok_pt],
                lambda e: e.transpose(PT1[:, 0:128], pt_sb[:], id_sb[:]), sT)
            tok_ptok = b.on("vector", [tok_tpt],
                lambda e: e.tensor_copy(ptok_sb[:], PT1[:, 0:128]), sV)
            def _vb(e, t=t):
                e.tensor_tensor(pblk_sb[:, t, :].rearrange("p (g q) -> p g q", g=8),
                                ptok_sb[:, 0:64].unsqueeze(1).broadcast_to([128, 8, 64]),
                                mk5_sb[:].rearrange("p (g q) -> p g q", g=8), OP.mult)
                last = None
                for c in range(4):
                    last = e.tensor_tensor(pblkT_sb[:, t, c, :], pt_sb[:], mkt_sb[:, c, :],
                                           OP.mult)
                return last
            pblk_toks[t] = b.on("vector", [tok_ptok, tok_pw], _vb, sV)

        dbg_dump("d_ln2", ln2_sb[:], [ln2_toks[-1]])
        dbg_dump("d_pblk", pblk_sb[:], [pblk_toks[-1]])
        dbg_dump("d_pblkt", pblkT_sb[:], [pblk_toks[-1]])
        if stage < 8:
            fin(); stack_cm.close(); return nc

        barrier8 = [spill_toks[-1], pblk_toks[-1], sT.tok(), sV.tok(), sS.tok(), sG.tok()]
        PFULL = [PA, PB, PC]
        ev_toks = []
        for t in range(NTL):
            for cp in range(NCH // 2):
                j = t * (NCH // 2) + cp
                Tt = PFULL[j % 3]
                free_tok = ev_toks[j - 3] if j >= 3 else None
                def _g(e, t=t, cp=cp, Tt=Tt):
                    e.matmul(Tt[:, 0:512], ln2_sb[:, t, ts(2 * cp, 128)], pblk_sb[:, t, :],
                             start=True, stop=True, skip_group_check=True)
                    return e.matmul(Tt[:, 512:1024], ln2_sb[:, t, ts(2 * cp + 1, 128)],
                                    pblk_sb[:, t, :], start=True, stop=True,
                                    skip_group_check=True)
                tok_g = b.on("tensor", barrier8 + [pblk_toks[t], ln2_toks[t], free_tok], _g, sT)
                ev_toks.append(b.on("vector", [tok_g],
                    lambda e, t=t, cp=cp, Tt=Tt: e.tensor_copy(
                        selt_sb[:, t, 2 * cp:2 * cp + 2, :],
                        Tt[:].rearrange("p (c q) -> p c q", c=2)), sV))
        tok_selt = [sV.tok(), sS.tok()]
        dbg_dump("d_selt", selt_sb[:], tok_selt)
        if stage < 9:
            fin(); stack_cm.close(); return nc

        f1_mm_toks, relu_toks = [], []
        blk_last_mm = []
        for blkI in range(8):
            buf = blkI % 2
            free_tok = blk_last_mm[blkI - 2] if blkI >= 2 else None
            tok_w = b.on("gpsimd", [free_tok] + tok_selt,
                lambda e, blkI=blkI, buf=buf: e.dma_start(
                    out=f1big_sb[:, buf, :, :],
                    in_=f1_h[8*blkI:8*blkI+8, :, :].transpose([1, 0, 2])), dF[buf], 16)
            for e8 in range(8):
                es = 8 * blkI + e8
                psl = [PA[0:64, 0:128], PA[0:64, 512:640],
                       PB[0:64, 0:128], PB[0:64, 512:640]][es % 4]
                def _mf(e, es=es, buf=buf, e8=e8, psl=psl):
                    last = None
                    for c in range(NCH):
                        last = e.matmul(psl, f1big_sb[:, buf, e8, ts(c, 64)],
                                        selt_sb[:, :, c, es:es + 64 * 7 + 1:64],
                                        start=(c == 0), stop=(c == NCH - 1),
                                        skip_group_check=True)
                    return last
                prev_relu = relu_toks[es - 4] if es >= 4 else None
                tok_mf = b.on("tensor", [tok_w, prev_relu] + tok_selt, _mf, sT)
                f1_mm_toks.append(tok_mf)
                relu_toks.append(b.on("vector", [tok_mf],
                    lambda e, es=es, psl=psl: e.tensor_scalar(inner_sb[:, es, :], psl,
                                                              fbt_sb[:, es:es+1], 0.0,
                                                              OP.add, OP.max), sV))
            blk_last_mm.append(f1_mm_toks[-1])
        tok_inner = relu_toks[-1]
        dbg_dump("d_inner", inner_sb[:], [tok_inner])
        if stage < 10:
            fin(); stack_cm.close(); return nc

        barrier10 = [tok_inner, f1_mm_toks[-1], sT.tok()]
        fin_toks = [None] * NTL
        outdma_toks = [None] * NTL
        for h in range(2):
            f2ev_toks, f2mm_toks = [], []
            blk2_last = []
            for blkI in range(8):
                buf = blkI % 2
                free_tok = blk2_last[blkI - 2] if blkI >= 2 else None
                tok_w = b.on("gpsimd", [free_tok] + barrier10,
                    lambda e, blkI=blkI, buf=buf, h=h: e.dma_start(
                        out=f2big_sb[:, buf, :, :],
                        in_=f2_h[8*blkI:8*blkI+8, :, ts(h, 512)].transpose([1, 0, 2])), dG2[buf], 16)
                for e8 in range(8):
                    es = 8 * blkI + e8
                    pj = es // 2
                    Tt = [PA, PB][pj % 2]
                    prev_ev = f2ev_toks[pj - 2] if (es % 2 == 0 and pj >= 2) else None
                    tok_m2 = b.on("tensor", [tok_w, prev_ev] + barrier10,
                        lambda e, es=es, buf=buf, e8=e8, Tt=Tt: e.matmul(
                            Tt[:, 512 * (es % 2):512 * (es % 2) + 512], inner_sb[:, es, :],
                            f2big_sb[:, buf, e8, :], start=True,
                            stop=True, skip_group_check=True), sT)
                    f2mm_toks.append(tok_m2)
                    if es % 2 == 1:
                        src = Tt[:].rearrange("p (c q) -> p c q", c=2)
                        if pj % 2 == 0:
                            f2ev_toks.append(b.on("vector", [tok_m2],
                                lambda e, es=es, src=src: e.tensor_copy(
                                    inter_sb[:, es - 1:es + 1, :], src), sV))
                        else:
                            f2ev_toks.append(b.on("scalar", [tok_m2],
                                lambda e, es=es, src=src: e.activation(
                                    inter_sb[:, es - 1:es + 1, :], src, AF.Copy), sS))
                blk2_last.append(f2mm_toks[-1])
            tok_inter = [sV.tok(), sS.tok()]
            if h == 0:
                dbg_dump("d_inter", inter_sb[:], tok_inter)
            stk_mm_all = []
            for t in range(NTL):
                tok_x2re = b.on("scalar", [dSP.tok(), fin_toks[t - 4] if t >= 4 else None],
                    lambda e, t=t, h=h: e.dma_start(out=x2re_sb[:, t % 4, :],
                                                    in_=out_h[ts(t, 128), ts(h, 512)]), dXR[t % 4], 16)
                for c in range(4):
                    g0 = 8 * t + 2 * c
                    i = 4 * t + c
                    sbuf_i = i % 8
                    prev_use = stk_mm_all[i - 8] if i >= 8 else None
                    qeng = "sync" if c % 2 == 0 else "scalar"
                    tok_rl = b.on(qeng, tok_inter + [prev_use],
                        lambda e, g0=g0, sbuf_i=sbuf_i: e.dma_start(out=stack_sb[:, sbuf_i, :],
                                                                    in_=inter_sb[g0:g0+2, :, :]),
                        dRL[sbuf_i], 16)
                    tok_ms = b.on("tensor", [tok_rl, pblk_toks[t],
                                             fin_toks[t - 2] if t >= 2 else None],
                        lambda e, t=t, c=c, sbuf_i=sbuf_i: e.matmul(
                            PC[:, ts(t % 2, 512)], pblkT_sb[:, t, c, :],
                            stack_sb[:, sbuf_i, :], start=(c == 0),
                            stop=(c == 3), skip_group_check=True), sT)
                    stk_mm_all.append(tok_ms)
                tok_scat = stk_mm_all[-1]
                tok_fin = b.on("vector", [tok_scat, tok_x2re, outdma_toks[t - 4] if t >= 4 else None],
                    lambda e, t=t: e.tensor_tensor(fin_sb[:, t % 4, :], PC[:, ts(t % 2, 512)],
                                                   x2re_sb[:, t % 4, :], OP.add), sV)
                fin_toks[t] = tok_fin
                outdma_toks[t] = b.on("sync", [tok_fin],
                    lambda e, t=t, h=h: e.dma_start(out=out_h[ts(t, 128), ts(h, 512)],
                                                    in_=fin_sb[:, t % 4, :]), dOU[t % 4], 16)
            barrier10 = [tok_inner, sT.tok(), sV.tok(), sS.tok()] + [d.tok() for d in dRL]

        fin()
        stack_cm.close()
    return nc


def _prep(inputs):
    x = np.asarray(inputs["L0_attn_x"], dtype=np.float32)
    qkv_w = np.ascontiguousarray(np.asarray(inputs["L0_attn_qkv_w"], dtype=np.float32))
    qkv_b = np.asarray(inputs["L0_attn_qkv_b"], dtype=np.float32)
    proj_w = np.asarray(inputs["L0_attn_proj_w"], dtype=np.float32)
    proj_b = np.asarray(inputs["L0_attn_proj_b"], dtype=np.float32)
    pm = np.asarray(inputs["L0_attn_projection_matrix"], dtype=np.float32)
    ctrl = np.asarray(inputs["L0_ffn_controller"], dtype=np.float32)
    f1 = np.asarray(inputs["L0_ffn_f1"], dtype=np.float32)
    f2 = np.asarray(inputs["L0_ffn_f2"], dtype=np.float32)
    fb = np.asarray(inputs["L0_ffn_bias"], dtype=np.float32)

    wqkv = np.ascontiguousarray(
        qkv_w[0:2048].reshape(16, 128, 8, 128).transpose(0, 3, 2, 1))
    wv = np.ascontiguousarray(
        qkv_w[2048:3072].reshape(8, 128, 8, 128).transpose(3, 2, 0, 1))
    qb2d = np.ascontiguousarray(qkv_b[0:2048].reshape(16, 128).T)
    pw = np.ascontiguousarray(proj_w.reshape(1024, 8, 128).transpose(2, 1, 0))
    pbt = np.ascontiguousarray(np.broadcast_to(proj_b[None, :], (128, 1024)))
    vbt = np.ascontiguousarray(np.broadcast_to(qkv_b[2048:3072][None, :], (128, 1024)))
    pm2 = np.zeros((128, 128), dtype=np.float32)
    pm2[0:64, 0:64] = pm.T
    pm2[64:128, 64:128] = pm.T
    C = ctrl.reshape(1024, 64)
    c2 = np.ascontiguousarray(
        np.concatenate([C, C], axis=1).reshape(8, 128, 128).transpose(1, 0, 2))
    f1h = np.ascontiguousarray(
        f1.reshape(1024, 64, 64).transpose(1, 0, 2)
          .reshape(64, 8, 128, 64).transpose(0, 2, 1, 3)
          .reshape(64, 128, 512))
    f2h = np.ascontiguousarray(f2.reshape(64, 64, 1024))
    fbt = np.ascontiguousarray(fb.reshape(64, 64).T)
    identb = np.eye(128, dtype=np.float32)
    mask512 = np.zeros((128, 512), dtype=np.float32)
    for g8 in range(8):
        mask512[g8*16:(g8+1)*16, g8*64:(g8+1)*64] = 1.0
    maskt = np.zeros((128, 4, 128), dtype=np.float32)
    for c in range(4):
        maskt[0:64, c, 32*c:32*c+16] = 1.0
        maskt[64:128, c, 32*c+16:32*c+32] = 1.0

    shared = dict(wqkv=wqkv, wv=wv, qb=qb2d, pw=pw, pbt=pbt, vbt=vbt, pm2=pm2,
                  c2=c2, f1h=f1h, f2h=f2h, fbt=fbt, identb=identb,
                  mask512=mask512, maskt=maskt)
    return x, shared


def kernel(**inputs):
    x, shared = _prep(inputs)
    nc = build_nc(stage=10)
    in_maps = [dict(shared, x=np.ascontiguousarray(x[c])) for c in range(8)]
    res = run_bass_kernel_spmd(nc, in_maps, core_ids=list(range(8)))
    out = np.stack([r["out"] for r in res.results], axis=0)
    return out.astype(np.float32)
